# version 12
# baseline (speedup 1.0000x reference)
import numpy as np
import ml_dtypes
import concourse.bass as bass
import concourse.mybir as mybir
from concourse.bass_utils import run_bass_kernel_spmd
from contextlib import ExitStack

F32 = mybir.dt.float32
BF16 = mybir.dt.bfloat16
AF = mybir.ActivationFunctionType
ALU = mybir.AluOpType
AX = mybir.AxisListType

PHASE_LIMIT = 5
SUBSET = "p1 v0 v1 k0 k1 q0 q1 bar"
P1_TILES = 20
P3_QB = 4
P3_H = 16
P3_LVL = 8
P3_MODE = 0
DBG_SCRATCH = False

D = 2048
KT = 16
H = 16
TOK = 2048
TOKH = 2560
QB = 4
NE = 32
GUT = 1536
ITK = 6
NG = 2
GT = 1024
SCALE = 128.0 ** -0.5
ESHIFT = -4.0


class Sem:
    def __init__(self, handle, name):
        self.h = handle
        self.name = name
        self.v = 0


class Prog:
    ENGINES = ("sync", "gpsimd", "tensor", "vector", "scalar")

    def __init__(self, nc, es):
        self.nc = nc
        self.es = es
        self.ops = {e: [] for e in self.ENGINES}
        self._last_wait = {e: {} for e in self.ENGINES}

    def sem(self, name):
        return Sem(self.es.enter_context(self.nc.semaphore(name)), name)

    DRAIN_INC = ("vector", "scalar")

    def op(self, eng, fn, waits=(), inc=None, n=1):
        real = []
        lw = self._last_wait[eng]
        for s, v in waits:
            if v <= 0 or lw.get(s.name, 0) >= v:
                continue
            lw[s.name] = v
            real.append((s, v))
        post = None
        if inc is not None:
            inc.v += n
            post = inc.v
        self.ops[eng].append((tuple(real), fn, inc, n))
        return post

    def dma(self, eng, fn, waits=(), inc=None):
        return self.op(eng, fn, waits=waits, inc=inc, n=16)

    def barrier(self, esem, extra=()):
        targets = {}
        for e in self.ENGINES:
            s = esem[e]
            self.op(e, lambda eng: eng.drain(), inc=s)
            targets[e] = s.v
        exvals = [(s, s.v) for s in extra]
        for e in self.ENGINES:
            w = [(esem[o], targets[o]) for o in self.ENGINES if o != e] + exvals
            self.op(e, lambda eng: eng.nop(), waits=w)

    def realize(self, block):
        def mk(name):
            drain_inc = name in self.DRAIN_INC
            def body(eng):
                for waits, fn, inc, n in self.ops[name]:
                    for s, v in waits:
                        eng.wait_ge(s.h, v)
                    ins = fn(eng)
                    if inc is not None:
                        if drain_inc and n == 1:
                            eng.drain(fusable=True).then_inc(inc.h, n)
                        else:
                            ins.then_inc(inc.h, n)
            return body
        block.sync(mk("sync"))
        block.gpsimd(mk("gpsimd"))
        block.tensor(mk("tensor"))
        block.vector(mk("vector"))
        block.scalar(mk("scalar"))


def build():
    nc = bass.Bass("TRN2", target_bir_lowering=False, debug=False)
    di = {}
    def din(name, shape, dt=F32):
        di[name] = nc.dram_tensor(name, shape, dt, kind="ExternalInput").ap()
        return di[name]
    dx = din("x", [TOKH, D])
    dwq = din("wq", [D, D]); dwk = din("wk", [D, D])
    dwv = din("wv", [D, D]); dwo = din("wo", [D, D])
    dgate = din("gate", [D, 32])
    dgu = din("gu", [NE, D, GUT])
    ddw = din("dw", [NE, ITK * 128, D])
    dropc = din("ropec", [128, TOKH]); drops = din("ropes", [128, TOKH])
    dmask = din("maskt", [QB, 1024, 512], BF16)
    did16 = din("id16m", [128, 128], BF16)
    did32 = din("id32m", [128, 128])
    dones16 = din("ones16", [128, 1], BF16)
    dones32 = din("ones32", [1, 128])
    dones32c = din("ones32c", [128, 1])
    dout = nc.dram_tensor("out", [TOK, D], F32, kind="ExternalOutput").ap()
    KIND = "ExternalOutput" if DBG_SCRATCH else "Internal"
    dqh = nc.dram_tensor("qh", [H, 128, TOK], BF16, kind=KIND).ap()
    dkh = nc.dram_tensor("kh", [H, 128, TOKH], BF16, kind=KIND).ap()
    dvh = nc.dram_tensor("vh", [20, 128, D], BF16, kind=KIND).ap()
    davh = nc.dram_tensor("avh", [KT, 128, TOK], BF16, kind=KIND).ap()
    dxatt = nc.dram_tensor("xatt", [KT, 128, TOK], F32, kind=KIND).ap()
    dcombt = nc.dram_tensor("combt", [NG, 32, GT], F32, kind=KIND).ap()
    dln1t = (nc.dram_tensor("ln1t", [128, KT, TOKH], BF16, kind="ExternalOutput").ap()
             if DBG_SCRATCH else None)
    dstat = (nc.dram_tensor("statd", [20, 128, 12], F32, kind="ExternalOutput").ap()
             if DBG_SCRATCH else None)

    with ExitStack() as es:
        P = Prog(nc, es)
        pe_s = P.sem("pe"); dve_s = P.sem("dve"); act_s = P.sem("act")
        sy_s = P.sem("sy"); gp_s = P.sem("gp")
        qc = P.sem("qc")
        l0 = P.sem("l0"); l1 = P.sem("l1")
        w0 = P.sem("w0"); w1 = P.sem("w1")
        s0 = P.sem("s0"); s1 = P.sem("s1")
        s2 = P.sem("s2"); s3 = P.sem("s3")
        qx = P.sem("qx"); qce = P.sem("qce"); qmisc = P.sem("qmisc")
        qout = P.sem("qout")
        a0 = P.sem("a0"); a1 = P.sem("a1")
        qrt = P.sem("qrt")
        esem = {"sync": sy_s, "gpsimd": gp_s, "tensor": pe_s,
                "vector": dve_s, "scalar": act_s}
        dmas = (qc, l0, l1, w0, w1, s0, s1, s2, s3, qx, qce, qmisc, qout,
                a0, a1, qrt)
        blk = es.enter_context(nc.Block())
        sb = lambda name, shape, dt: es.enter_context(nc.sbuf_tensor(name, shape, dt))

        id16 = sb("id16", [128, 128], BF16)
        id32 = sb("id32", [128, 128], F32)
        ones16 = sb("ones16s", [128, 1], BF16)
        ones32 = sb("ones32s", [1, 128], F32)
        ones32c = sb("ones32cs", [128, 1], F32)
        P.dma("sync", lambda e: e.dma_start(out=id16[:], in_=did16[:, :]), inc=qc)
        P.dma("sync", lambda e: e.dma_start(out=id32[:], in_=did32[:, :]), inc=qc)
        P.dma("sync", lambda e: e.dma_start(out=ones16[:], in_=dones16[:, :]), inc=qc)
        P.dma("sync", lambda e: e.dma_start(out=ones32[:], in_=dones32[:, :]), inc=qc)
        P.dma("sync", lambda e: e.dma_start(out=ones32c[:], in_=dones32c[:, :]), inc=qc)

        with ExitStack() as e12:
            sb2 = lambda name, shape, dt: e12.enter_context(nc.sbuf_tensor(name, shape, dt))
            ps2 = lambda name, shape, dt: e12.enter_context(nc.psum_tensor(name, shape, dt))
            ln1T = sb2("ln1T", [128, KT, TOKH], BF16)
            xstg = [sb2("xstg0", [128, D], F32)]
            lnst = [sb2("lnst0", [128, D], BF16)]
            statD = sb2("statD", [128, 8], F32)
            statA = sb2("statA", [128, 4], F32)
            wbufA = sb2("wbufA", [128, KT, 1024], BF16)
            wbufB = sb2("wbufB", [128, KT, 1024], BF16)
            ropc = sb2("ropc", [128, TOKH], BF16)
            rops = sb2("rops", [128, TOKH], BF16)
            tmpa = sb2("tmpa", [64, 4, 512], BF16)
            tmpb = sb2("tmpb", [64, 4, 512], BF16)
            kstg = [sb2(f"kstg{i}", [128, 4, 512], BF16) for i in range(2)]
            kstr = [sb2(f"kstr{i}", [128, 512], BF16) for i in range(2)]
            vstg = [sb2(f"vstg{i}", [128, 1024], BF16) for i in range(2)]
            pT = ps2("pT", [128, 8, 128], BF16)
            pV = ps2("pV", [128, 1024], F32)
            pK = ps2("pK", [128, 4, 512], F32)
            pKr = ps2("pKr", [128, 512], F32)

            P.dma("gpsimd", lambda e: e.dma_start(out=ropc[:], in_=dropc[:, :]), inc=qrt)
            P.dma("gpsimd", lambda e: e.dma_start(out=rops[:], in_=drops[:, :]), inc=qrt)
            const_ready = qc.v
            rope_ready = qrt.v

            wready = {"v0": 0, "v1": 0, "k0": 0, "k1": 0, "q0": 0, "q1": 0}
            wsem = {}
            def load_w(buf, src, colslice, tag, wait=()):
                q = w0 if buf is wbufA else w1
                last = 0
                for k in range(KT):
                    last = P.dma("gpsimd", lambda e, k=k, buf=buf, src=src, cs=colslice:
                              e.dma_start(out=buf[:, k, :],
                                          in_=src[k*128:(k+1)*128, cs[0]:cs[1]]),
                              waits=wait, inc=q)
                wready[tag] = last
                wsem[tag] = q
            if "v0" in SUBSET:
                load_w(wbufA, dwv, (0, 1024), "v0")
            vfree = [(dve_s, 0)]
            vst_dma = [0, 0]
            kfree_g = [(dve_s, 0), (dve_s, 0)]
            kst_g = [0, 0]

            def v_tile(vh_i, buf, tag, tt):
                first = True
                last_mm = 0
                for k in range(KT):
                    for vs in range(2):
                        w = ([vfree[0], (wsem[tag], wready[tag]),
                              (dve_s, tile_done[tt])] if first else ())
                        last_mm = P.op("tensor",
                            lambda e, buf=buf, tt=tt, k=k, vs=vs:
                            e.matmul(pV[:, vs*512:(vs+1)*512],
                                     ln1T[:, k, tt*128:tt*128+128],
                                     buf[:, k, vs*512:(vs+1)*512],
                                     start=(k == 0), stop=(k == KT-1)),
                            waits=w, inc=(pe_s if (k == KT-1 and vs == 1) else None))
                        first = False
                sq = s0 if tt % 2 == 0 else s1
                vev = P.op("vector", lambda e, tt=tt: e.tensor_copy(
                    vstg[tt % 2][:], pV[:]),
                    waits=[(pe_s, last_mm), (sq, vst_dma[tt % 2])], inc=dve_s)
                vfree[0] = (dve_s, vev)
                vst_dma[tt % 2] = P.dma("scalar",
                    lambda e, tt=tt, vh_i=vh_i: e.dma_start(
                        out=dvh[tt, :, vh_i*1024:(vh_i+1)*1024],
                        in_=vstg[tt % 2][:]),
                    waits=[(dve_s, vev)], inc=sq)

            def v_half(vh_i, buf, tag, prefetch):
                for tt in range(20):
                    v_tile(vh_i, buf, tag, tt)
                    if tt == 0 and prefetch is not None:
                        prefetch()

            def qk_mh(dst, buf, tag, mh, chunks, c_off, prefetch):
                groups = []
                cl = list(chunks)
                while cl:
                    groups.append(cl[:4])
                    cl = cl[4:]
                for mi in range(8):
                    m = mh * 8 + mi
                    for gi, grp in enumerate(groups):
                        big = len(grp) == 4
                        PSrc = pK if big else pKr
                        first = True
                        last_mm = 0
                        for ci, c in enumerate(grp):
                            for k in range(KT):
                                w = ([kfree_g[0 if big else 1],
                                      (wsem[tag], wready[tag]),
                                      (dve_s, ln_all)] if first else ())
                                out_ap = (PSrc[:, ci, :] if big else PSrc[:])
                                last_mm = P.op("tensor",
                                    lambda e, buf=buf, mi=mi, k=k, c=c, out_ap=out_ap:
                                    e.matmul(out_ap, buf[:, k, mi*128:mi*128+128],
                                             ln1T[:, k, c*512:c*512+512],
                                             start=(k == 0), stop=(k == KT-1)),
                                    waits=w, inc=(pe_s if k == KT-1 and ci == len(grp)-1
                                                  else None))
                                first = False
                        b = m % 2
                        c0 = grp[0]
                        nw = len(grp) * 512
                        if big:
                            stg = kstg[b]
                            qsb = lnst[0].rearrange("p (g n) -> p g n", n=512)
                            psrc = pK[:, :, :]
                            qe = qsb[0:64, :, :]
                            qo = qsb[64:128, :, :]
                            stg_e = stg[0:64, :, :]
                            stg_o = stg[64:128, :, :]
                            ta = tmpa[:, :, :]
                            tb = tmpb[:, :, :]
                        else:
                            stg = kstr[b]
                            qsb = lnst[0]
                            psrc = pKr[:]
                            qe = qsb[0:64, 0:512]
                            qo = qsb[64:128, 0:512]
                            stg_e = stg[0:64, :]
                            stg_o = stg[64:128, :]
                            ta = tmpa[:, 0, 0:512]
                            tb = tmpb[:, 0, 0:512]
                        ssq = s2 if b == 0 else s3
                        sq_dma = kst_g
                        if big:
                            cs = (lambda t, c0=c0, nw=nw:
                                  t[:, c0*512:c0*512+nw].rearrange(
                                      "p (g n) -> p g n", n=512))
                        else:
                            cs = (lambda t, c0=c0:
                                  t[:, c0*512:c0*512+512])
                        qdst = (qsb[:, :, :] if big else qsb[:, 0:512])
                        P.op("vector", lambda e, psrc=psrc, qdst=qdst: e.tensor_copy(
                            qdst, psrc),
                            waits=[(pe_s, last_mm), (ssq, sq_dma[b]),
                                   (qrt, rope_ready), (dve_s, ln_all)])
                        P.op("vector", lambda e: e.drain())
                        P.op("vector", lambda e, qe=qe, cs=cs, ta=ta: e.tensor_mul(
                            ta, qe, cs(ropc[0:64, :])))
                        P.op("vector", lambda e, qo=qo, cs=cs, tb=tb: e.tensor_mul(
                            tb, qo, cs(rops[64:128, :])))
                        P.op("vector", lambda e: e.drain())
                        P.op("vector", lambda e, stg_e=stg_e, ta=ta, tb=tb: e.tensor_sub(
                            stg_e, ta, tb))
                        P.op("vector", lambda e: e.drain())
                        P.op("vector", lambda e, qe=qe, cs=cs, ta=ta: e.tensor_mul(
                            ta, qe, cs(rops[0:64, :])))
                        P.op("vector", lambda e, qo=qo, cs=cs, tb=tb: e.tensor_mul(
                            tb, qo, cs(ropc[64:128, :])))
                        P.op("vector", lambda e: e.drain())
                        vev = P.op("vector", lambda e, stg_o=stg_o, ta=ta, tb=tb: e.tensor_add(
                            stg_o, ta, tb), inc=dve_s)
                        kfree_g[0 if big else 1] = (dve_s, vev)
                        sq_dma[b] = P.dma("scalar",
                            lambda e, m=m, c0=c0, nw=nw, dst=dst, c_off=c_off, stg=stg:
                            e.dma_start(
                                out=dst[m, :, (c0-c_off)*512:(c0-c_off)*512+nw],
                                in_=stg[:]),
                            waits=[(dve_s, vev)], inc=ssq)
                    if mi == 0 and prefetch is not None:
                        prefetch()


            tile_done = [0] * 20
            sq_done = [0] * 20
            pt_free = (dve_s, 0)
            prev_vrs = 0
            prev_vvar = P.op("vector", lambda e: e.memset(statD[:], 0.0),
                             inc=dve_s)
            prev_vnorm = 0
            for tt in range(P1_TILES if "p1" in SUBSET else 0):
                b = 0
                xw = []
                if tt >= 1:
                    xw = [(dve_s, prev_vnorm), (act_s, sq_done[tt-1])]
                lq = l0 if tt % 2 == 0 else l1
                xl = P.dma("sync", lambda e, tt=tt, b=b: e.dma_start(
                    out=xstg[b][:], in_=dx[tt*128:(tt+1)*128, :]),
                    waits=xw, inc=lq)
                sq_done[tt] = P.op("scalar", lambda e, b=b: e.activation(
                    lnst[b][:], xstg[b][:], AF.Square, accum_out=statA[:, 0:1]),
                    waits=[(lq, xl), (dve_s, prev_vvar)] +
                          ([(pe_s, tr_done[tt-1])] if tt >= 1 else []),
                    inc=act_s)
                P.op("vector", lambda e, b=b: e.reduce_sum(
                    statD[:, 0:1], xstg[b][:], AX.X), waits=[(lq, xl)])
                P.op("vector", lambda e: e.tensor_scalar(
                    statD[:, 3:4], statA[:, 0:1], 1.0 / D, 1e-5,
                    ALU.mult, ALU.add),
                    waits=[(act_s, sq_done[tt]), (act_s, prev_vrs)])
                P.op("vector", lambda e: e.drain())
                P.op("vector", lambda e: e.tensor_scalar(
                    statD[:, 1:2], statD[:, 0:1], 1.0 / D, None, ALU.mult))
                P.op("vector", lambda e: e.drain())
                P.op("vector", lambda e: e.tensor_mul(
                    statD[:, 2:3], statD[:, 1:2], statD[:, 1:2]))
                P.op("vector", lambda e: e.drain())
                vvar = P.op("vector", lambda e: e.tensor_sub(
                    statD[:, 4:5], statD[:, 3:4], statD[:, 2:3]), inc=dve_s)
                prev_vvar = vvar
                vln = P.op("scalar", lambda e: e.activation(
                    statA[:, 1:2], statD[:, 4:5], AF.Ln),
                    waits=[(dve_s, vvar)], inc=act_s)
                vrs = P.op("scalar", lambda e: e.activation(
                    statA[:, 2:3], statA[:, 1:2], AF.Exp, scale=-0.5),
                    waits=[(act_s, vln)], inc=act_s)
                prev_vrs = vrs
                vnorm = P.op("vector", lambda e, b=b: e.tensor_scalar(
                    lnst[b][:], xstg[b][:], statD[:, 1:2], statA[:, 2:3],
                    ALU.subtract, ALU.mult), waits=[(act_s, vrs)], inc=dve_s)
                prev_vnorm = vnorm
                if tt == 0:
                    tr_done = [0] * 20
                last_tr = 0
                vcps = []
                for half in range(2):
                    for k8 in range(8):
                        k = half * 8 + k8
                        last_tr = P.op("tensor", lambda e, b=b, k=k, k8=k8: e.transpose(
                            pT[:, k8, :], lnst[b][:, k*128:(k+1)*128], id16[:]),
                            waits=([(dve_s, vnorm), (qc, const_ready), pt_free]
                                   if k8 == 0 else []),
                            inc=(pe_s if k8 == 7 else None))
                    vcp = P.op("vector", lambda e, tt=tt, half=half: e.tensor_copy(
                        ln1T[:, half*8:half*8+8, tt*128:tt*128+128], pT[:, :, :]),
                        waits=[(pe_s, last_tr)], inc=dve_s)
                    pt_free = (dve_s, vcp)
                    vcps.append(vcp)
                tr_done[tt] = last_tr
                tile_done[tt] = vcps[-1]
                if "v0" in SUBSET:
                    v_tile(0, wbufA, "v0", tt)
                    if tt == 0:
                        load_w(wbufB, dwv, (1024, 2048), "v1")
                if DBG_SCRATCH:
                    P.dma("sync", lambda e, tt=tt: e.dma_start(
                        out=dstat[tt, :, 0:8], in_=statD[:]),
                        waits=[(dve_s, vnorm)], inc=qx)
            ln_all = tile_done[-1]
            if DBG_SCRATCH:
                P.dma("sync", lambda e: e.dma_start(out=dln1t[:, :, :], in_=ln1T[:]),
                      waits=[(dve_s, ln_all)], inc=qx)
            _sections = {
                "v1": lambda: v_half(1, wbufB, "v1",
                    (lambda: load_w(wbufA, dwk, (0, 1024), "k0",
                                    wait=[(pe_s, pe_s.v)]))
                    if "k0" in SUBSET else None),
                "k0": lambda: qk_mh(dkh, wbufA, "k0", 0, range(5), 0,
                    (lambda: load_w(wbufB, dwk, (1024, 2048), "k1",
                                    wait=[(pe_s, pe_s.v)]))
                    if "k1" in SUBSET else None),
                "k1": lambda: qk_mh(dkh, wbufB, "k1", 1, range(5), 0,
                    (lambda: load_w(wbufA, dwq, (0, 1024), "q0",
                                    wait=[(pe_s, pe_s.v)]))
                    if "q0" in SUBSET else None),
                "q0": lambda: qk_mh(dqh, wbufA, "q0", 0, range(1, 5), 1,
                    (lambda: load_w(wbufB, dwq, (1024, 2048), "q1",
                                    wait=[(pe_s, pe_s.v)]))
                    if "q1" in SUBSET else None),
                "q1": lambda: qk_mh(dqh, wbufB, "q1", 1, range(1, 5), 1, None),
            }



            for sec in ("v1", "k0", "k1", "q0", "q1"):
                if sec in SUBSET:
                    _sections[sec]()

            if "bar" in SUBSET:
                P.barrier(esem, extra=dmas)

        if PHASE_LIMIT >= 3:
          with ExitStack() as e3:
            sb3 = lambda name, shape, dt: e3.enter_context(nc.sbuf_tensor(name, shape, dt))
            ps3 = lambda name, shape, dt: e3.enter_context(nc.psum_tensor(name, shape, dt))
            qbuf = sb3("qbuf", [128, H, 512], BF16)
            kbuf = sb3("kbuf", [128, H, 1024], BF16)
            vbuf = sb3("vbuf", [128, 8, D], BF16)
            mbuf = sb3("mbuf", [128, 8, 512], BF16)
            expT = [sb3(f"expT{i}", [128, 8, 512], BF16) for i in range(2)]
            rvec0 = [sb3(f"rvec0{i}", [1, 512], F32) for i in range(2)]
            rvec = [sb3(f"rvec{i}", [1, 512], F32) for i in range(2)]
            rbs = [sb3(f"rbs{i}", [128, 512], F32) for i in range(2)]
            avstg = [sb3(f"avstg{i}", [128, 512], BF16) for i in range(2)]
            SA = ps3("SA", [128, 4, 512], F32)
            SB = ps3("SB", [128, 4, 512], F32)

            qkv_done = [ (s0, s0.v), (s1, s1.v), (s2, s2.v), (s3, s3.v) ]
            av_dma = [0, 0]
            ps_free = [(act_s, 0), (act_s, 0)]
            tail_free = [(dve_s, 0), (dve_s, 0)]
            pending = None

            def emit_tail(pi, PS, last_av, vl4, h, qb):
                vbc = P.op("tensor", lambda e, PS=PS, pi=pi: e.matmul(
                    PS[:, 2, :], ones32[:], rvec[pi][:], start=True, stop=True),
                    waits=[(act_s, vl4)], inc=pe_s)
                vcp = P.op("scalar", lambda e, PS=PS, pi=pi: e.copy(
                    rbs[pi][:], PS[:, 2, :]),
                    waits=[(pe_s, vbc)], inc=act_s)
                ps_free[pi] = (act_s, vcp)
                vav = P.op("vector", lambda e, PS=PS, pi=pi: e.tensor_mul(
                    avstg[pi][:], PS[:, 0, :], rbs[pi][:]),
                    waits=[(pe_s, last_av), (act_s, vcp),
                           (s0 if pi == 0 else s1, av_dma[pi])],
                    inc=dve_s)
                av_dma[pi] = P.dma("scalar", lambda e, h=h, qb=qb, pi=pi:
                    e.dma_start(out=davh[h, :, qb*512:qb*512+512],
                                in_=avstg[pi][:]),
                    waits=[(dve_s, vav)], inc=(s0 if pi == 0 else s1))
                tail_free[pi] = (dve_s, vav)

            for qb in range(min(QB, P3_QB)):
                prev_pe = pe_s.v
                w0_ = qkv_done + [(pe_s, prev_pe)]
                for h in range(H):
                    P.dma("sync", lambda e, h=h, qb=qb: e.dma_start(
                        out=qbuf[:, h, :], in_=dqh[h, :, qb*512:qb*512+512]),
                        waits=w0_, inc=qx)
                    P.dma("sync", lambda e, h=h, qb=qb: e.dma_start(
                        out=kbuf[:, h, :], in_=dkh[h, :, qb*512:qb*512+1024]),
                        waits=w0_, inc=qx)
                for j in range(8):
                    P.dma("sync", lambda e, j=j, qb=qb: e.dma_start(
                        out=vbuf[:, j, :], in_=dvh[qb*4 + j, :, :]),
                        waits=w0_, inc=qx)
                ld_done = 0
                for mh in range(2):
                    ld_done = P.dma("sync", lambda e, qb=qb, mh=mh: e.dma_start(
                        out=mbuf[:, mh*4:(mh+1)*4, :],
                        in_=dmask[qb, mh*512:(mh+1)*512, :]
                            .rearrange("(jt p) i -> p jt i", p=128)),
                        waits=w0_, inc=qx)
                for h in range(H):
                    pi = h % 2
                    PS = SA if pi == 0 else SB
                    eT = expT[pi]
                    exp_done = [0, 0]
                    for w in range(2):
                        last_mm = 0
                        for j4 in range(4):
                            j = w * 4 + j4
                            wts = []
                            if j4 == 0 and w == 0:
                                wts = [(qx, ld_done), ps_free[pi], tail_free[pi]]
                            elif j4 == 0:
                                wts = [(act_s, exp_done[0])]
                            P.op("tensor", lambda e, PS=PS, j4=j4, j=j, h=h:
                                e.matmul(PS[:, j4, :],
                                         kbuf[:, h, j*128:j*128+128],
                                         qbuf[:, h, :], start=True, stop=False),
                                waits=wts)
                            last_mm = P.op("tensor", lambda e, PS=PS, j4=j4, j=j:
                                e.matmul(PS[:, j4, :], id16[:], mbuf[:, j, :],
                                         start=False, stop=True),
                                waits=[(qc, const_ready)] if j4 == 0 and h == 0 else [],
                                inc=(pe_s if j4 == 3 else None))
                        exp_done[w] = P.op("scalar", lambda e, PS=PS, eT=eT, w=w:
                            e.activation(eT[:, w*4:w*4+4, :], PS[:, :, :],
                                         AF.Exp, scale=SCALE),
                            waits=[(pe_s, last_mm)], inc=act_s)
                    last_av = 0
                    for j in range(8):
                        last_av = P.op("tensor", lambda e, PS=PS, eT=eT, j=j, h=h:
                            e.matmul(PS[:, 0, :], vbuf[:, j, h*128:h*128+128],
                                     eT[:, j, :], start=(j == 0), stop=(j == 7)),
                            waits=([(act_s, exp_done[1])] if j == 0 else []),
                            inc=(pe_s if j == 7 else None))
                    last_se = 0
                    for j in range(8):
                        last_se = P.op("tensor", lambda e, PS=PS, eT=eT, j=j:
                            e.matmul(PS[0:1, 1, :], ones16[:], eT[:, j, :],
                                     start=(j == 0), stop=(j == 7)),
                            inc=(pe_s if j == 7 else None))
                    vl3 = P.op("scalar", lambda e, PS=PS, pi=pi: e.activation(
                        rvec0[pi][:], PS[0:1, 1, :], AF.Ln),
                        waits=[(pe_s, last_se)], inc=act_s)
                    vl4 = P.op("scalar", lambda e, pi=pi: e.activation(
                        rvec[pi][:], rvec0[pi][:], AF.Exp, scale=-1.0),
                        waits=[(act_s, vl3)], inc=act_s)
                    if pending is not None:
                        emit_tail(*pending)
                    pending = (pi, PS, last_av, vl4, h, qb)
                if pending is not None:
                    emit_tail(*pending)
                    pending = None

            P.barrier(esem, extra=dmas)

        if PHASE_LIMIT >= 4:
          with ExitStack() as e4:
            sb4 = lambda name, shape, dt: e4.enter_context(nc.sbuf_tensor(name, shape, dt))
            ps4 = lambda name, shape, dt: e4.enter_context(nc.psum_tensor(name, shape, dt))
            wob = sb4("wob", [128, KT, D], BF16)
            avb = [sb4(f"avb{i}", [128, KT, 512], BF16) for i in range(2)]
            xst = [sb4(f"xst{i}", [128, D], F32) for i in range(2)]
            xT = sb4("xT", [128, KT, 512], F32)
            ost = [sb4(f"ost{i}", [128, 512], F32) for i in range(2)]
            pX = ps4("pX", [128, 4, 128], F32)
            pW = [ps4(f"pW{i}", [128, 512], F32) for i in range(2)]

            for k in range(KT):
                P.dma("gpsimd", lambda e, k=k: e.dma_start(
                    out=wob[:, k, :], in_=dwo[k*128:(k+1)*128, :]), inc=qmisc)
            wo_ready = qmisc.v
            av_done = [(s0, s0.v), (s1, s1.v)]

            ost_dma = [0, 0]
            pw_free = [(dve_s, 0), (dve_s, 0)]
            px_free = (dve_s, 0)
            avb_free = [0, 0]
            xst_free = [0, 0]
            for c in range(4):
                cb = c % 2
                avq = a0 if cb == 0 else a1
                avl = 0
                for q4 in range(4):
                    avl = P.dma("sync", lambda e, c=c, cb=cb, q4=q4: e.dma_start(
                        out=avb[cb][:, q4*4:(q4+1)*4, :],
                        in_=davh[q4*4:(q4+1)*4, :, c*512:c*512+512]
                            .rearrange("h p t -> p h t")),
                        waits=av_done + [(pe_s, avb_free[cb])], inc=avq)
                for t4 in range(4):
                    tb = t4 % 2
                    xq = l0 if tb == 0 else l1
                    xl = P.dma("sync", lambda e, c=c, t4=t4, tb=tb: e.dma_start(
                        out=xst[tb][:],
                        in_=dx[512 + c*512 + t4*128: 512 + c*512 + t4*128 + 128, :]),
                        waits=[(pe_s, xst_free[tb])], inc=xq)
                    for fb in range(4):
                        vtr = 0
                        for f4 in range(4):
                            f = fb * 4 + f4
                            vtr = P.op("tensor", lambda e, tb=tb, f=f, f4=f4:
                                e.transpose(pX[:, f4, :],
                                            xst[tb][:, f*128:f*128+128], id32[:]),
                                waits=([(xq, xl), px_free] if f4 == 0 else []),
                                inc=(pe_s if f4 == 3 else None))
                        vev = P.op("vector", lambda e, fb=fb, t4=t4: e.tensor_copy(
                            xT[:, fb*4:fb*4+4, t4*128:t4*128+128], pX[:, :, :]),
                            waits=[(pe_s, vtr)], inc=dve_s)
                        px_free = (dve_s, vev)
                    xst_free[tb] = pe_s.v
                xT_ready = px_free[1]
                for o in range(KT):
                    ob = o % 2
                    last_mm = 0
                    for k in range(KT):
                        last_mm = P.op("tensor", lambda e, o=o, k=k, cb=cb, ob=ob:
                            e.matmul(pW[ob][:], wob[:, k, o*128:o*128+128],
                                     avb[cb][:, k, :], start=(k == 0),
                                     stop=(k == KT-1)),
                            waits=([(qmisc, wo_ready), (avq, avl), pw_free[ob]]
                                   if k == 0 else []),
                            inc=(pe_s if k == KT-1 else None))
                    osq = s0 if ob == 0 else s1
                    vev = P.op("vector", lambda e, o=o, ob=ob: e.tensor_add(
                        ost[ob][:], pW[ob][:], xT[:, o, :]),
                        waits=[(pe_s, last_mm), (osq, ost_dma[ob]),
                               (dve_s, xT_ready)], inc=dve_s)
                    pw_free[ob] = (dve_s, vev)
                    ost_dma[ob] = P.dma("scalar", lambda e, o=o, c=c, ob=ob:
                        e.dma_start(out=dxatt[o, :, c*512:c*512+512],
                                    in_=ost[ob][:]),
                        waits=[(dve_s, vev)], inc=osq)
                avb_free[cb] = pe_s.v

            P.barrier(esem, extra=dmas)

        if PHASE_LIMIT >= 5:
          with ExitStack() as e5:
            sb5 = lambda name, shape, dt: e5.enter_context(nc.sbuf_tensor(name, shape, dt))
            ps5 = lambda name, shape, dt: e5.enter_context(nc.psum_tensor(name, shape, dt))
            acc = sb5("acc", [128, KT, GT], F32)
            ybuf = sb5("ybuf", [128, KT, GT], BF16)
            hbuf = sb5("hbuf", [128, ITK, GT], BF16)
            guw = sb5("guw", [128, KT, GUT], BF16)
            dww = sb5("dww", [128, ITK, D], BF16)
            cesb = sb5("cesb", [128, GT], F32)
            t1 = sb5("t1", [128, GT], F32)
            gtb = sb5("gtb", [128, KT, 32], BF16)
            aux = sb5("aux", [32, 3, GT], F32)
            lgT = sb5("lgT", [128, 256], F32)
            l8 = sb5("l8", [128, 8], F32)
            e8b = sb5("e8b", [128, 8], F32)
            cbuf = sb5("cbuf", [128, 8], F32)
            csh = sb5("csh", [128, 8], F32)
            kfb = sb5("kfb", [128, 8], F32)
            selb = sb5("selb", [128, 8], F32)
            zer8 = sb5("zer8", [128, 8], F32)
            sjnk = sb5("sjnk", [128, 2], F32)
            m32b = sb5("m32b", [128, 32], F32)
            e32b = sb5("e32b", [128, 32], F32)
            w32b = sb5("w32b", [128, 32], F32)
            scl = sb5("scl", [128, 4], F32)
            comb = sb5("comb", [128, 256], F32)
            outst = sb5("outst", [128, 512], F32)
            pg = [ps5(f"pg{i}", [128, GT], F32) for i in range(2)]
            pu = [ps5(f"pu{i}", [128, GT], F32) for i in range(2)]

            P.op("vector", lambda e: e.memset(zer8[:], 0.0))
            P.op("vector", lambda e: e.memset(sjnk[:], 0.0))
            for k in range(KT):
                P.dma("gpsimd", lambda e, k=k: e.dma_start(
                    out=gtb[:, k, :], in_=dgate[k*128:(k+1)*128, :]), inc=qmisc)
            gate_ready = qmisc.v

            guw_ready = 0
            dww_ready = 0
            last_gu_mm = 0
            last_dn_mm = 0

            def load_gu(e_):
                nonlocal guw_ready
                for k in range(KT):
                    guw_ready = P.dma("gpsimd", lambda en, e_=e_, k=k: en.dma_start(
                        out=guw[:, k, :], in_=dgu[e_, k*128:(k+1)*128, :]),
                        waits=[(pe_s, last_gu_mm)], inc=w0)

            def load_dw(e_):
                nonlocal dww_ready
                for k in range(ITK):
                    dww_ready = P.dma("gpsimd", lambda en, e_=e_, k=k: en.dma_start(
                        out=dww[:, k, :], in_=ddw[e_, k*128:(k+1)*128, :]),
                        waits=[(pe_s, last_dn_mm)], inc=w1)

            load_gu(0)
            load_dw(0)

            free_ps = {}
            hready = [(dve_s, 0)] * ITK
            ce_free = (dve_s, 0)
            t1_free = (dve_s, 0)
            acc_free_pe = 0
            y_done = 0
            out_dma = 0
            comb_st = 0

            for g in range(NG):
                acc_ld = 0
                for k in range(KT):
                    acc_ld = P.dma("sync", lambda e, k=k, g=g: e.dma_start(
                        out=acc[:, k, :], in_=dxatt[k, :, g*GT:(g+1)*GT]),
                        waits=[(pe_s, acc_free_pe)], inc=qx)
                last_st = 0
                sq_pe = 0
                for k in range(KT):
                    P.op("vector", lambda e, k=k: e.tensor_copy(
                        ybuf[:, k, :], acc[:, k, :]),
                        waits=[(qx, acc_ld), (pe_s, last_gu_mm)],
                        inc=(dve_s if k == KT-1 else None))
                    vsq = P.op("scalar", lambda e, k=k: e.activation(
                        t1[:], acc[:, k, :], AF.Square),
                        waits=[(qx, acc_ld), (pe_s, sq_pe), t1_free], inc=act_s)
                    for hf in range(2):
                        P.op("tensor", lambda e, k=k, hf=hf: e.matmul(
                            pg[0][0:1, hf*512:hf*512+512], ones32c[:],
                            acc[:, k, hf*512:hf*512+512],
                            start=(k == 0), stop=(k == KT-1)),
                            waits=[(qx, acc_ld)] +
                                  ([free_ps.get(id(pg[0]), (dve_s, 0))]
                                   if k == 0 and hf == 0 else []))
                        v_ = P.op("tensor", lambda e, k=k, hf=hf: e.matmul(
                            pu[0][0:1, hf*512:hf*512+512], ones32c[:],
                            t1[:, hf*512:hf*512+512],
                            start=(k == 0), stop=(k == KT-1)),
                            waits=[(act_s, vsq)] +
                                  ([free_ps.get(id(pu[0]), (dve_s, 0))]
                                   if k == 0 and hf == 0 else []),
                            inc=(pe_s if hf == 1 else None))
                        if hf == 1:
                            last_st = v_
                    sq_pe = last_st
                ycp_done = dve_s.v
                P.op("vector", lambda e: e.tensor_scalar(
                    aux[0:1, 0, :], pg[0][0:1, 0:GT], 1.0 / D, None, ALU.mult),
                    waits=[(pe_s, last_st)])
                P.op("vector", lambda e: e.tensor_scalar(
                    aux[0:1, 2, :], pu[0][0:1, 0:GT], 1.0 / D, 1e-5,
                    ALU.mult, ALU.add))
                P.op("vector", lambda e: e.drain())
                P.op("vector", lambda e: e.tensor_mul(
                    aux[0:1, 1, :], aux[0:1, 0, :], aux[0:1, 0, :]))
                P.op("vector", lambda e: e.drain())
                vvar = P.op("vector", lambda e: e.tensor_sub(
                    aux[0:1, 1, :], aux[0:1, 2, :], aux[0:1, 1, :]), inc=dve_s)
                free_ps[id(pg[0])] = (dve_s, vvar)
                free_ps[id(pu[0])] = (dve_s, vvar)
                vln2 = P.op("scalar", lambda e: e.activation(
                    aux[0:1, 2, :], aux[0:1, 1, :], AF.Ln),
                    waits=[(dve_s, vvar)], inc=act_s)
                vrs = P.op("scalar", lambda e: e.activation(
                    aux[0:1, 1, :], aux[0:1, 2, :], AF.Exp, scale=-0.5),
                    waits=[(act_s, vln2)], inc=act_s)
                vbb = P.op("vector", lambda e: e.scalar_tensor_tensor(
                    aux[0:1, 2, :], aux[0:1, 0, :], -1.0, aux[0:1, 1, :],
                    ALU.mult, ALU.mult), waits=[(act_s, vrs)], inc=dve_s)
                vbc = 0
                for hf in range(2):
                    P.op("tensor", lambda e, hf=hf: e.matmul(
                        pg[1][:, hf*512:hf*512+512], ones32[:],
                        aux[0:1, 1, hf*512:hf*512+512], start=True, stop=True),
                        waits=[(dve_s, vbb)] +
                              ([free_ps.get(id(pg[1]), (dve_s, 0))] if hf == 0 else []))
                    vbc = P.op("tensor", lambda e, hf=hf: e.matmul(
                        pu[1][:, hf*512:hf*512+512], ones32[:],
                        aux[0:1, 2, hf*512:hf*512+512], start=True, stop=True),
                        waits=([free_ps.get(id(pu[1]), (dve_s, 0))] if hf == 0 else []),
                        inc=(pe_s if hf == 1 else None))
                for k in range(KT):
                    P.op("vector", lambda e, k=k: e.tensor_mul(
                        t1[:], acc[:, k, :], pg[1][:]),
                        waits=([(pe_s, vbc), t1_free, (pe_s, last_st)]
                               if k == 0 else []))
                    P.op("vector", lambda e: e.drain())
                    y_done = P.op("vector", lambda e, k=k: e.tensor_add(
                        ybuf[:, k, :], t1[:], pu[1][:]), inc=dve_s)
                    P.op("vector", lambda e: e.drain())
                free_ps[id(pg[1])] = (dve_s, y_done)
                free_ps[id(pu[1])] = (dve_s, y_done)
                t1_free = (dve_s, y_done)
                last_lg = 0
                for hf in range(2):
                    for k in range(KT):
                        last_lg = P.op("tensor", lambda e, hf=hf, k=k: e.matmul(
                            pg[0][0:32, hf*512:hf*512+512], gtb[:, k, :],
                            ybuf[:, k, hf*512:hf*512+512],
                            start=(k == 0), stop=(k == KT-1)),
                            waits=([(qmisc, gate_ready), (dve_s, y_done),
                                    free_ps.get(id(pg[0]), (dve_s, 0))]
                                   if k == 0 and hf == 0 else []),
                            inc=(pe_s if (k == KT-1 and hf == 1) else None))
                vlg = P.op("scalar", lambda e: e.copy(aux[0:32, 0, :], pg[0][0:32, 0:GT]),
                           waits=[(pe_s, last_lg), (qx, comb_st)], inc=act_s)
                free_ps[id(pg[0])] = (act_s, vlg)
                vtr = 0
                for c8 in range(8):
                    vtr = P.op("tensor", lambda e, c8=c8: e.transpose(
                        pu[0][:, c8*32:c8*32+32], aux[0:32, 0, c8*128:c8*128+128],
                        id32[0:32, 0:32]),
                        waits=([(act_s, vlg),
                                free_ps.get(id(pu[0]), (dve_s, 0))]
                               if c8 == 0 else []),
                        inc=(pe_s if c8 == 7 else None))
                vlgt = P.op("vector", lambda e: e.tensor_copy(
                    lgT[:], pu[0][:, 0:256]),
                    waits=[(pe_s, vtr)], inc=dve_s)
                free_ps[id(pu[0])] = (dve_s, vlgt)
                comb_last = 0
                rt_reader = 0
                def SPC():
                    P.op("vector", lambda e: e.drain())
                for c8 in range(8):
                    P.op("vector", lambda e, c8=c8: e.max(
                        l8[:], lgT[:, c8*32:c8*32+32]),
                        waits=[(dve_s, vlgt), (act_s, rt_reader)])
                    SPC()
                    vng = P.op("vector", lambda e: e.tensor_scalar(
                        scl[:, 0:1], l8[:, 0:1], -1.0, None, ALU.mult), inc=dve_s)
                    ve8 = P.op("scalar", lambda e: e.activation(
                        e8b[:], l8[:], AF.Exp, bias=scl[:, 0:1]),
                        waits=[(dve_s, vng)], inc=act_s)
                    P.op("vector", lambda e: e.tensor_tensor_scan(
                        cbuf[:], e8b[:], zer8[:], 0.0, ALU.add, ALU.add),
                        waits=[(act_s, ve8)])
                    SPC()
                    P.op("vector", lambda e: e.tensor_scalar(
                        scl[:, 1:2], cbuf[:, 7:8], 0.8, None, ALU.mult))
                    P.op("vector", lambda e: e.memset(csh[:, 0:1], 0.0))
                    P.op("vector", lambda e: e.tensor_copy(
                        csh[:, 1:8], cbuf[:, 0:7]))
                    SPC()
                    P.op("vector", lambda e: e.tensor_single_scalar(
                        kfb[:], csh[:], scl[:, 1:2], ALU.is_lt))
                    SPC()
                    P.op("vector", lambda e: e.tensor_scalar(
                        selb[:], kfb[:], -1e30, 1e30, ALU.mult, ALU.add))
                    SPC()
                    P.op("vector", lambda e: e.tensor_add(
                        selb[:], selb[:], l8[:]))
                    SPC()
                    P.op("vector", lambda e: e.tensor_reduce(
                        scl[:, 2:3], selb[:], AX.X, ALU.min))
                    SPC()
                    SPC()
                    vms = P.op("vector", lambda e, c8=c8: e.tensor_single_scalar(
                        m32b[:], lgT[:, c8*32:c8*32+32], scl[:, 2:3], ALU.is_ge),
                        inc=dve_s)
                    ve32 = P.op("scalar", lambda e, c8=c8: e.activation(
                        e32b[:], lgT[:, c8*32:c8*32+32], AF.Exp, bias=scl[:, 0:1]),
                        waits=[(dve_s, vms)], inc=act_s)
                    rt_reader = ve32
                    SPC()
                    P.op("vector", lambda e: e.tensor_mul(
                        w32b[:], e32b[:], m32b[:]), waits=[(act_s, ve32)])
                    SPC()
                    P.op("vector", lambda e: e.reduce_sum(
                        scl[:, 3:4], w32b[:], AX.X))
                    SPC()
                    P.op("vector", lambda e: e.reciprocal(
                        scl[:, 3:4], scl[:, 3:4]))
                    SPC()
                    SPC()
                    comb_last = P.op("vector", lambda e, c8=c8: e.tensor_scalar(
                        comb[:, c8*32:c8*32+32], w32b[:], scl[:, 3:4], None,
                        ALU.mult), inc=dve_s)
                def emit_comb_store(g=None):
                    nonlocal comb_st
                    vtr2 = 0
                    for c8 in range(8):
                        pdst = pg[1] if c8 < 4 else pu[1]
                        vtr2 = P.op("tensor", lambda e, c8=c8, pdst=pdst: e.transpose(
                            pdst[0:32, (c8 % 4)*128:(c8 % 4)*128+128],
                            comb[:, c8*32:c8*32+32], id32[:]),
                            waits=([(dve_s, comb_last),
                                    free_ps.get(id(pg[1]), (dve_s, 0)),
                                    free_ps.get(id(pu[1]), (dve_s, 0))]
                                   if c8 == 0 else []),
                            inc=(pe_s if c8 == 7 else None))
                    vct = P.op("scalar", lambda e: e.copy(
                        aux[0:32, 0, 0:512], pg[1][0:32, 0:512]),
                        waits=[(pe_s, vtr2)], inc=act_s)
                    vct2 = P.op("scalar", lambda e: e.copy(
                        aux[0:32, 0, 512:1024], pu[1][0:32, 0:512]), inc=act_s)
                    free_ps[id(pg[1])] = (act_s, vct)
                    free_ps[id(pu[1])] = (act_s, vct2)
                    comb_st = P.dma("sync", lambda e, g=g: e.dma_start(
                        out=dcombt[g, :, :], in_=aux[0:32, 0, :]),
                        waits=[(act_s, vct2)], inc=qx)

                for e_ in range(NE):
                    if e_ > 0:
                        ce_ready = P.dma("sync", lambda en, g=g, e_=e_: en.dma_start(
                            out=cesb[:],
                            in_=dcombt[g, e_:e_+1, :].partition_broadcast(128)),
                            waits=[ce_free, (qx, comb_st)], inc=qce)
                    for jp in range(ITK):
                        p = jp & 1
                        for tile, col0 in ((pg[p], jp*128), (pu[p], 768 + jp*128)):
                            wts0 = [free_ps.get(id(tile), (dve_s, 0)),
                                    (w0, guw_ready), (dve_s, y_done)]
                            first = True
                            for k in range(KT):
                                for hf in range(2):
                                    v = P.op("tensor",
                                        lambda en, tile=tile, col0=col0, k=k, hf=hf:
                                        en.matmul(tile[:, hf*512:hf*512+512],
                                                  guw[:, k, col0:col0+128],
                                                  ybuf[:, k, hf*512:hf*512+512],
                                                  start=(k == 0), stop=(k == KT-1)),
                                        waits=(wts0 if first else ()),
                                        inc=(pe_s if (k == KT-1 and hf == 1)
                                             else None))
                                    if k == KT-1 and hf == 1:
                                        last_gu_mm = v
                                    first = False
                        if e_ == 0 and jp == 0:
                            emit_comb_store(g)
                            ce_ready = P.dma("sync", lambda en, g=g: en.dma_start(
                                out=cesb[:],
                                in_=dcombt[g, 0:1, :].partition_broadcast(128)),
                                waits=[ce_free, (qx, comb_st)], inc=qce)
                        va = P.op("scalar", lambda en, p=p: en.activation(
                            t1[:], pg[p][:], AF.Silu),
                            waits=[(pe_s, last_gu_mm), t1_free], inc=act_s)
                        free_ps[id(pg[p])] = (act_s, va)
                        vd = P.op("vector", lambda en, p=p: en.tensor_mul(
                            t1[:], t1[:], pu[p][:]),
                            waits=[(act_s, va)], inc=dve_s)
                        free_ps[id(pu[p])] = (dve_s, vd)
                        vh = P.op("vector", lambda en, jp=jp: en.tensor_mul(
                            hbuf[:, jp, :], t1[:], cesb[:]),
                            waits=[(qce, ce_ready), (pe_s, last_dn_mm)], inc=dve_s)
                        t1_free = (dve_s, vh)
                        hready[jp] = (dve_s, vh)
                    ce_free = hready[ITK - 1]
                    if e_ + 1 < NE:
                        load_gu(e_ + 1)
                    pseq = [pg[0], pu[0], pg[1], pu[1]]
                    for o in range(KT):
                        tile = pseq[o % 4]
                        wts0 = [free_ps.get(id(tile), (dve_s, 0)),
                                (w1, dww_ready)]
                        first = True
                        for k in range(ITK):
                            for hf in range(2):
                                wts = list(wts0) if first else []
                                wts.append(hready[k])
                                v = P.op("tensor",
                                    lambda en, tile=tile, o=o, k=k, hf=hf:
                                    en.matmul(tile[:, hf*512:hf*512+512],
                                              dww[:, k, o*128:o*128+128],
                                              hbuf[:, k, hf*512:hf*512+512],
                                              start=(k == 0), stop=(k == ITK-1)),
                                    waits=wts,
                                    inc=(pe_s if (k == ITK-1 and hf == 1)
                                         else None))
                                if k == ITK-1 and hf == 1:
                                    last_dn_mm = v
                                first = False
                        vacc = P.op("vector", lambda en, tile=tile, o=o:
                            en.tensor_add(acc[:, o, :], acc[:, o, :], tile[:]),
                            waits=[(pe_s, last_dn_mm)], inc=dve_s)
                        free_ps[id(tile)] = (dve_s, vacc)
                    if e_ + 1 < NE:
                        load_dw(e_ + 1)
                    elif g + 1 < NG:
                        load_gu(0)
                        load_dw(0)
                acc_done = dve_s.v
                for tt in range(8):
                    for fb in range(4):
                        pb = pg[fb % 2]
                        vtr = 0
                        for f4 in range(4):
                            f = fb * 4 + f4
                            vtr = P.op("tensor", lambda e, f=f, f4=f4, tt=tt, pb=pb:
                                e.transpose(pb[:, f4*128:f4*128+128],
                                            acc[:, f, tt*128:tt*128+128], id32[:]),
                                waits=([(dve_s, acc_done),
                                        free_ps.get(id(pb), (dve_s, 0))]
                                       if f4 == 0 else []),
                                inc=(pe_s if f4 == 3 else None))
                        vev = P.op("vector", lambda e, pb=pb: e.tensor_copy(
                            outst[:], pb[:, 0:512]),
                            waits=[(pe_s, vtr), (qout, out_dma)], inc=dve_s)
                        free_ps[id(pb)] = (dve_s, vev)
                        out_dma = P.dma("sync", lambda e, g=g, tt=tt, fb=fb:
                            e.dma_start(out=dout[g*GT + tt*128: g*GT + tt*128 + 128,
                                                 fb*512:fb*512+512],
                                        in_=outst[:]),
                            waits=[(dve_s, vev)], inc=qout)
                acc_free_pe = pe_s.v

        P.barrier(esem, extra=dmas)

        P.realize(blk)
    return nc


_NC_CACHE = {}


def kernel(L0_attn_x, L0_attn_Wq, L0_attn_Wk, L0_attn_Wv, L0_attn_Wo,
           L0_ffn_gate_weight, L0_ffn_gate_up_proj, L0_ffn_down_proj):
    x = np.ascontiguousarray(np.asarray(L0_attn_x, dtype=np.float32))
    B, S, _ = x.shape
    idx = np.empty(D, np.int64)
    for h in range(H):
        base = h * 128
        idx[base:base+64] = base + 2 * np.arange(64)
        idx[base+64:base+128] = base + 2 * np.arange(64) + 1
    wq = np.ascontiguousarray(np.asarray(L0_attn_Wq).T[:, idx], np.float32)
    wk = np.ascontiguousarray(np.asarray(L0_attn_Wk).T[:, idx], np.float32)
    wv = np.ascontiguousarray(np.asarray(L0_attn_Wv).T, np.float32)
    wo = np.ascontiguousarray(np.asarray(L0_attn_Wo).T, np.float32)
    gate = np.ascontiguousarray(np.asarray(L0_ffn_gate_weight).T, np.float32)
    gu = np.ascontiguousarray(
        np.asarray(L0_ffn_gate_up_proj).transpose(0, 2, 1), np.float32)
    dw = np.ascontiguousarray(
        np.asarray(L0_ffn_down_proj).transpose(0, 2, 1), np.float32)
    inv_freq = 1.0 / 10000.0 ** (np.arange(0, 128, 2, dtype=np.float64) / 128.0)
    ident32 = np.eye(128, dtype=np.float32)
    ident16 = np.eye(128).astype(ml_dtypes.bfloat16)
    ones16 = np.ones((128, 1), ml_dtypes.bfloat16)
    ones32 = np.ones((1, 128), np.float32)
    i_ = np.arange(512)[None, :]
    j_ = np.arange(1024)[:, None]
    base_m = (j_ > i_) & (j_ <= i_ + 512)
    shift = ESHIFT / SCALE
    m_full = np.where(base_m, shift, -1e30).astype(np.float32)
    m_n0 = np.where(base_m & (j_ >= 512), shift, -1e30).astype(np.float32)

    nc = _NC_CACHE.get("nc")
    if nc is None:
        nc = build()
        _NC_CACHE["nc"] = nc

    in_maps = []
    for c in range(8):
        seq, half = c // 2, c % 2
        own = x[seq, half*TOK:(half+1)*TOK]
        halo = (np.zeros((512, D), np.float32) if half == 0
                else x[seq, TOK-512:TOK])
        x_all = np.ascontiguousarray(np.concatenate([halo, own], axis=0))
        pos = np.maximum(half*TOK - 512 + np.arange(TOKH), 0).astype(np.float64)
        ang = pos[None, :] * inv_freq[:, None]
        ropec = np.concatenate([np.cos(ang)] * 2, 0).astype(np.float32)
        ropes = np.concatenate([np.sin(ang)] * 2, 0).astype(np.float32)
        masks = np.stack([m_n0 if (half == 0 and qb == 0) else m_full
                          for qb in range(QB)]).astype(ml_dtypes.bfloat16)
        in_maps.append({
            "x": x_all, "wq": wq, "wk": wk, "wv": wv, "wo": wo, "gate": gate,
            "gu": gu, "dw": dw, "ropec": ropec, "ropes": ropes, "maskt": masks,
            "id16m": ident16, "id32m": ident32, "ones16": ones16,
            "ones32": ones32, "ones32c": np.ones((128, 1), np.float32),
        })
    res = run_bass_kernel_spmd(nc, in_maps, core_ids=list(range(8)))
    out = np.empty((B, S, D), np.float32)
    for c in range(8):
        seq, half = c // 2, c % 2
        out[seq, half*TOK:(half+1)*TOK] = res.results[c]["out"]
    return out


# version 13
# speedup vs baseline: 1.1355x; 1.1355x over previous
import numpy as np
import ml_dtypes
import concourse.bass as bass
import concourse.mybir as mybir
from concourse.bass_utils import run_bass_kernel_spmd
from contextlib import ExitStack

F32 = mybir.dt.float32
BF16 = mybir.dt.bfloat16
AF = mybir.ActivationFunctionType
ALU = mybir.AluOpType
AX = mybir.AxisListType

PHASE_LIMIT = 5
SUBSET = "p1 v0 v1 k0 k1 q0 q1 bar"
P1_TILES = 20
P3_QB = 4
P3_H = 16
P3_LVL = 8
P3_MODE = 0
DBG_SCRATCH = False

D = 2048
KT = 16
H = 16
TOK = 2048
TOKH = 2560
QB = 4
NE = 32
GUT = 1536
ITK = 6
NG = 2
GT = 1024
SCALE = 128.0 ** -0.5
ESHIFT = -4.0


class Sem:
    def __init__(self, handle, name):
        self.h = handle
        self.name = name
        self.v = 0


class Prog:
    ENGINES = ("sync", "gpsimd", "tensor", "vector", "scalar")

    def __init__(self, nc, es):
        self.nc = nc
        self.es = es
        self.ops = {e: [] for e in self.ENGINES}
        self._last_wait = {e: {} for e in self.ENGINES}

    def sem(self, name):
        return Sem(self.es.enter_context(self.nc.semaphore(name)), name)

    DRAIN_INC = ("vector", "scalar")

    def op(self, eng, fn, waits=(), inc=None, n=1):
        real = []
        lw = self._last_wait[eng]
        for s, v in waits:
            if v <= 0 or lw.get(s.name, 0) >= v:
                continue
            lw[s.name] = v
            real.append((s, v))
        post = None
        if inc is not None:
            inc.v += n
            post = inc.v
        self.ops[eng].append((tuple(real), fn, inc, n))
        return post

    def dma(self, eng, fn, waits=(), inc=None):
        return self.op(eng, fn, waits=waits, inc=inc, n=16)

    def barrier(self, esem, extra=()):
        targets = {}
        for e in self.ENGINES:
            s = esem[e]
            self.op(e, lambda eng: eng.drain(), inc=s)
            targets[e] = s.v
        exvals = [(s, s.v) for s in extra]
        for e in self.ENGINES:
            w = [(esem[o], targets[o]) for o in self.ENGINES if o != e] + exvals
            self.op(e, lambda eng: eng.nop(), waits=w)

    def realize(self, block):
        def mk(name):
            drain_inc = name in self.DRAIN_INC
            def body(eng):
                for waits, fn, inc, n in self.ops[name]:
                    for s, v in waits:
                        eng.wait_ge(s.h, v)
                    ins = fn(eng)
                    if inc is not None:
                        if drain_inc and n == 1:
                            eng.drain(fusable=True).then_inc(inc.h, n)
                        else:
                            ins.then_inc(inc.h, n)
            return body
        block.sync(mk("sync"))
        block.gpsimd(mk("gpsimd"))
        block.tensor(mk("tensor"))
        block.vector(mk("vector"))
        block.scalar(mk("scalar"))


def build():
    nc = bass.Bass("TRN2", target_bir_lowering=False, debug=False)
    di = {}
    def din(name, shape, dt=F32):
        di[name] = nc.dram_tensor(name, shape, dt, kind="ExternalInput").ap()
        return di[name]
    dx = din("x", [TOKH, D])
    dwq = din("wq", [D, D]); dwk = din("wk", [D, D])
    dwv = din("wv", [D, D]); dwo = din("wo", [D, D])
    dgate = din("gate", [D, 32])
    dgu = din("gu", [NE, D, GUT])
    ddw = din("dw", [NE, ITK * 128, D])
    dropc = din("ropec", [128, TOKH]); drops = din("ropes", [128, TOKH])
    dmask = din("maskt", [QB, 1024, 512], BF16)
    did16 = din("id16m", [128, 128], BF16)
    did32 = din("id32m", [128, 128])
    dones16 = din("ones16", [128, 1], BF16)
    dones32 = din("ones32", [1, 128])
    dones32c = din("ones32c", [128, 1])
    dout = nc.dram_tensor("out", [TOK, D], F32, kind="ExternalOutput").ap()
    KIND = "ExternalOutput" if DBG_SCRATCH else "Internal"
    dqh = nc.dram_tensor("qh", [H, 128, TOK], BF16, kind=KIND).ap()
    dkh = nc.dram_tensor("kh", [H, 128, TOKH], BF16, kind=KIND).ap()
    dvh = nc.dram_tensor("vh", [20, 128, D], BF16, kind=KIND).ap()
    davh = nc.dram_tensor("avh", [KT, 128, TOK], BF16, kind=KIND).ap()
    dxatt = nc.dram_tensor("xatt", [KT, 128, TOK], F32, kind=KIND).ap()
    dcombt = nc.dram_tensor("combt", [NG, 32, GT], F32, kind=KIND).ap()
    dln1t = (nc.dram_tensor("ln1t", [128, KT, TOKH], BF16, kind="ExternalOutput").ap()
             if DBG_SCRATCH else None)
    dstat = (nc.dram_tensor("statd", [20, 128, 12], F32, kind="ExternalOutput").ap()
             if DBG_SCRATCH else None)

    with ExitStack() as es:
        P = Prog(nc, es)
        pe_s = P.sem("pe"); dve_s = P.sem("dve"); act_s = P.sem("act")
        sy_s = P.sem("sy"); gp_s = P.sem("gp")
        qc = P.sem("qc")
        l0 = P.sem("l0"); l1 = P.sem("l1")
        w0 = P.sem("w0"); w1 = P.sem("w1")
        s0 = P.sem("s0"); s1 = P.sem("s1")
        s2 = P.sem("s2"); s3 = P.sem("s3")
        qx = P.sem("qx"); qce = P.sem("qce"); qmisc = P.sem("qmisc")
        qout = P.sem("qout")
        a0 = P.sem("a0"); a1 = P.sem("a1")
        qrt = P.sem("qrt")
        esem = {"sync": sy_s, "gpsimd": gp_s, "tensor": pe_s,
                "vector": dve_s, "scalar": act_s}
        dmas = (qc, l0, l1, w0, w1, s0, s1, s2, s3, qx, qce, qmisc, qout,
                a0, a1, qrt)
        blk = es.enter_context(nc.Block())
        sb = lambda name, shape, dt: es.enter_context(nc.sbuf_tensor(name, shape, dt))

        id16 = sb("id16", [128, 128], BF16)
        id32 = sb("id32", [128, 128], F32)
        ones16 = sb("ones16s", [128, 1], BF16)
        ones32 = sb("ones32s", [1, 128], F32)
        ones32c = sb("ones32cs", [128, 1], F32)
        P.dma("sync", lambda e: e.dma_start(out=id16[:], in_=did16[:, :]), inc=qc)
        P.dma("sync", lambda e: e.dma_start(out=id32[:], in_=did32[:, :]), inc=qc)
        P.dma("sync", lambda e: e.dma_start(out=ones16[:], in_=dones16[:, :]), inc=qc)
        P.dma("sync", lambda e: e.dma_start(out=ones32[:], in_=dones32[:, :]), inc=qc)
        P.dma("sync", lambda e: e.dma_start(out=ones32c[:], in_=dones32c[:, :]), inc=qc)

        with ExitStack() as e12:
            sb2 = lambda name, shape, dt: e12.enter_context(nc.sbuf_tensor(name, shape, dt))
            ps2 = lambda name, shape, dt: e12.enter_context(nc.psum_tensor(name, shape, dt))
            ln1T = sb2("ln1T", [128, KT, TOKH], BF16)
            xstg = [sb2("xstg0", [128, D], F32)]
            lnst = [sb2("lnst0", [128, D], BF16)]
            statD = sb2("statD", [128, 8], F32)
            statA = sb2("statA", [128, 4], F32)
            wbufA = sb2("wbufA", [128, KT, 1024], BF16)
            wbufB = sb2("wbufB", [128, KT, 1024], BF16)
            ropc = sb2("ropc", [128, TOKH], BF16)
            rops = sb2("rops", [128, TOKH], BF16)
            tmpa = sb2("tmpa", [64, 4, 512], BF16)
            tmpb = sb2("tmpb", [64, 4, 512], BF16)
            kstg = [sb2(f"kstg{i}", [128, 4, 512], BF16) for i in range(2)]
            kstr = [sb2(f"kstr{i}", [128, 512], BF16) for i in range(2)]
            vstg = [sb2(f"vstg{i}", [128, 1024], BF16) for i in range(2)]
            pT = ps2("pT", [128, 8, 128], BF16)
            pV = ps2("pV", [128, 1024], F32)
            pK = ps2("pK", [128, 4, 512], F32)
            pKr = ps2("pKr", [128, 512], F32)

            P.dma("gpsimd", lambda e: e.dma_start(out=ropc[:], in_=dropc[:, :]), inc=qrt)
            P.dma("gpsimd", lambda e: e.dma_start(out=rops[:], in_=drops[:, :]), inc=qrt)
            const_ready = qc.v
            rope_ready = qrt.v

            wready = {"v0": 0, "v1": 0, "k0": 0, "k1": 0, "q0": 0, "q1": 0}
            wsem = {}
            def load_w(buf, src, colslice, tag, wait=()):
                q = w0 if buf is wbufA else w1
                last = 0
                for k in range(KT):
                    last = P.dma("gpsimd", lambda e, k=k, buf=buf, src=src, cs=colslice:
                              e.dma_start(out=buf[:, k, :],
                                          in_=src[k*128:(k+1)*128, cs[0]:cs[1]]),
                              waits=wait, inc=q)
                wready[tag] = last
                wsem[tag] = q
            if "v0" in SUBSET:
                load_w(wbufA, dwv, (0, 1024), "v0")
            vfree = [(dve_s, 0)]
            vst_dma = [0, 0]
            kfree_g = [(dve_s, 0), (dve_s, 0)]
            kst_g = [0, 0]

            def v_tile(vh_i, buf, tag, tt):
                first = True
                last_mm = 0
                for k in range(KT):
                    for vs in range(2):
                        w = ([vfree[0], (wsem[tag], wready[tag]),
                              (dve_s, tile_done[tt])] if first else ())
                        last_mm = P.op("tensor",
                            lambda e, buf=buf, tt=tt, k=k, vs=vs:
                            e.matmul(pV[:, vs*512:(vs+1)*512],
                                     ln1T[:, k, tt*128:tt*128+128],
                                     buf[:, k, vs*512:(vs+1)*512],
                                     start=(k == 0), stop=(k == KT-1)),
                            waits=w, inc=(pe_s if (k == KT-1 and vs == 1) else None))
                        first = False
                sq = s0 if tt % 2 == 0 else s1
                vev = P.op("vector", lambda e, tt=tt: e.tensor_copy(
                    vstg[tt % 2][:], pV[:]),
                    waits=[(pe_s, last_mm), (sq, vst_dma[tt % 2])], inc=dve_s)
                vfree[0] = (dve_s, vev)
                vst_dma[tt % 2] = P.dma("scalar",
                    lambda e, tt=tt, vh_i=vh_i: e.dma_start(
                        out=dvh[tt, :, vh_i*1024:(vh_i+1)*1024],
                        in_=vstg[tt % 2][:]),
                    waits=[(dve_s, vev)], inc=sq)

            def v_half(vh_i, buf, tag, prefetch):
                for tt in range(20):
                    v_tile(vh_i, buf, tag, tt)
                    if tt == 0 and prefetch is not None:
                        prefetch()

            def qk_mh(dst, buf, tag, mh, chunks, c_off, prefetch):
                groups = []
                cl = list(chunks)
                while cl:
                    groups.append(cl[:4])
                    cl = cl[4:]
                for mi in range(8):
                    m = mh * 8 + mi
                    for gi, grp in enumerate(groups):
                        big = len(grp) == 4
                        PSrc = pK if big else pKr
                        first = True
                        last_mm = 0
                        for ci, c in enumerate(grp):
                            for k in range(KT):
                                w = ([kfree_g[0 if big else 1],
                                      (wsem[tag], wready[tag]),
                                      (dve_s, ln_all)] if first else ())
                                out_ap = (PSrc[:, ci, :] if big else PSrc[:])
                                last_mm = P.op("tensor",
                                    lambda e, buf=buf, mi=mi, k=k, c=c, out_ap=out_ap:
                                    e.matmul(out_ap, buf[:, k, mi*128:mi*128+128],
                                             ln1T[:, k, c*512:c*512+512],
                                             start=(k == 0), stop=(k == KT-1)),
                                    waits=w, inc=(pe_s if k == KT-1 and ci == len(grp)-1
                                                  else None))
                                first = False
                        b = m % 2
                        c0 = grp[0]
                        nw = len(grp) * 512
                        if big:
                            stg = kstg[b]
                            qsb = lnst[0].rearrange("p (g n) -> p g n", n=512)
                            psrc = pK[:, :, :]
                            qe = qsb[0:64, :, :]
                            qo = qsb[64:128, :, :]
                            stg_e = stg[0:64, :, :]
                            stg_o = stg[64:128, :, :]
                            ta = tmpa[:, :, :]
                            tb = tmpb[:, :, :]
                        else:
                            stg = kstr[b]
                            qsb = lnst[0]
                            psrc = pKr[:]
                            qe = qsb[0:64, 0:512]
                            qo = qsb[64:128, 0:512]
                            stg_e = stg[0:64, :]
                            stg_o = stg[64:128, :]
                            ta = tmpa[:, 0, 0:512]
                            tb = tmpb[:, 0, 0:512]
                        ssq = s2 if b == 0 else s3
                        sq_dma = kst_g
                        if big:
                            cs = (lambda t, c0=c0, nw=nw:
                                  t[:, c0*512:c0*512+nw].rearrange(
                                      "p (g n) -> p g n", n=512))
                        else:
                            cs = (lambda t, c0=c0:
                                  t[:, c0*512:c0*512+512])
                        qdst = (qsb[:, :, :] if big else qsb[:, 0:512])
                        P.op("vector", lambda e, psrc=psrc, qdst=qdst: e.tensor_copy(
                            qdst, psrc),
                            waits=[(pe_s, last_mm), (ssq, sq_dma[b]),
                                   (qrt, rope_ready), (dve_s, ln_all)])
                        P.op("vector", lambda e: e.drain())
                        P.op("vector", lambda e, qe=qe, cs=cs, ta=ta: e.tensor_mul(
                            ta, qe, cs(ropc[0:64, :])))
                        P.op("vector", lambda e, qo=qo, cs=cs, tb=tb: e.tensor_mul(
                            tb, qo, cs(rops[64:128, :])))
                        P.op("vector", lambda e: e.drain())
                        P.op("vector", lambda e, stg_e=stg_e, ta=ta, tb=tb: e.tensor_sub(
                            stg_e, ta, tb))
                        P.op("vector", lambda e: e.drain())
                        P.op("vector", lambda e, qe=qe, cs=cs, ta=ta: e.tensor_mul(
                            ta, qe, cs(rops[0:64, :])))
                        P.op("vector", lambda e, qo=qo, cs=cs, tb=tb: e.tensor_mul(
                            tb, qo, cs(ropc[64:128, :])))
                        P.op("vector", lambda e: e.drain())
                        vev = P.op("vector", lambda e, stg_o=stg_o, ta=ta, tb=tb: e.tensor_add(
                            stg_o, ta, tb), inc=dve_s)
                        kfree_g[0 if big else 1] = (dve_s, vev)
                        sq_dma[b] = P.dma("scalar",
                            lambda e, m=m, c0=c0, nw=nw, dst=dst, c_off=c_off, stg=stg:
                            e.dma_start(
                                out=dst[m, :, (c0-c_off)*512:(c0-c_off)*512+nw],
                                in_=stg[:]),
                            waits=[(dve_s, vev)], inc=ssq)
                    if mi == 0 and prefetch is not None:
                        prefetch()


            tile_done = [0] * 20
            sq_done = [0] * 20
            pt_free = (dve_s, 0)
            prev_vrs = 0
            prev_vvar = P.op("vector", lambda e: e.memset(statD[:], 0.0),
                             inc=dve_s)
            prev_vnorm = 0
            for tt in range(P1_TILES if "p1" in SUBSET else 0):
                b = 0
                xw = []
                if tt >= 1:
                    xw = [(dve_s, prev_vnorm), (act_s, sq_done[tt-1])]
                lq = l0 if tt % 2 == 0 else l1
                xl = P.dma("sync", lambda e, tt=tt, b=b: e.dma_start(
                    out=xstg[b][:], in_=dx[tt*128:(tt+1)*128, :]),
                    waits=xw, inc=lq)
                sq_done[tt] = P.op("scalar", lambda e, b=b: e.activation(
                    lnst[b][:], xstg[b][:], AF.Square, accum_out=statA[:, 0:1]),
                    waits=[(lq, xl), (dve_s, prev_vvar)] +
                          ([(pe_s, tr_done[tt-1])] if tt >= 1 else []),
                    inc=act_s)
                P.op("vector", lambda e, b=b: e.reduce_sum(
                    statD[:, 0:1], xstg[b][:], AX.X), waits=[(lq, xl)])
                P.op("vector", lambda e: e.tensor_scalar(
                    statD[:, 3:4], statA[:, 0:1], 1.0 / D, 1e-5,
                    ALU.mult, ALU.add),
                    waits=[(act_s, sq_done[tt]), (act_s, prev_vrs)])
                P.op("vector", lambda e: e.drain())
                P.op("vector", lambda e: e.tensor_scalar(
                    statD[:, 1:2], statD[:, 0:1], 1.0 / D, None, ALU.mult))
                P.op("vector", lambda e: e.drain())
                P.op("vector", lambda e: e.tensor_mul(
                    statD[:, 2:3], statD[:, 1:2], statD[:, 1:2]))
                P.op("vector", lambda e: e.drain())
                vvar = P.op("vector", lambda e: e.tensor_sub(
                    statD[:, 4:5], statD[:, 3:4], statD[:, 2:3]), inc=dve_s)
                prev_vvar = vvar
                vln = P.op("scalar", lambda e: e.activation(
                    statA[:, 1:2], statD[:, 4:5], AF.Ln),
                    waits=[(dve_s, vvar)], inc=act_s)
                vrs = P.op("scalar", lambda e: e.activation(
                    statA[:, 2:3], statA[:, 1:2], AF.Exp, scale=-0.5),
                    waits=[(act_s, vln)], inc=act_s)
                prev_vrs = vrs
                vnorm = P.op("vector", lambda e, b=b: e.tensor_scalar(
                    lnst[b][:], xstg[b][:], statD[:, 1:2], statA[:, 2:3],
                    ALU.subtract, ALU.mult), waits=[(act_s, vrs)], inc=dve_s)
                prev_vnorm = vnorm
                if tt == 0:
                    tr_done = [0] * 20
                last_tr = 0
                vcps = []
                for half in range(2):
                    for k8 in range(8):
                        k = half * 8 + k8
                        last_tr = P.op("tensor", lambda e, b=b, k=k, k8=k8: e.transpose(
                            pT[:, k8, :], lnst[b][:, k*128:(k+1)*128], id16[:]),
                            waits=([(dve_s, vnorm), (qc, const_ready), pt_free]
                                   if k8 == 0 else []),
                            inc=(pe_s if k8 == 7 else None))
                    vcp = P.op("vector", lambda e, tt=tt, half=half: e.tensor_copy(
                        ln1T[:, half*8:half*8+8, tt*128:tt*128+128], pT[:, :, :]),
                        waits=[(pe_s, last_tr)], inc=dve_s)
                    pt_free = (dve_s, vcp)
                    vcps.append(vcp)
                tr_done[tt] = last_tr
                tile_done[tt] = vcps[-1]
                if "v0" in SUBSET:
                    v_tile(0, wbufA, "v0", tt)
                    if tt == 0:
                        load_w(wbufB, dwv, (1024, 2048), "v1")
                if DBG_SCRATCH:
                    P.dma("sync", lambda e, tt=tt: e.dma_start(
                        out=dstat[tt, :, 0:8], in_=statD[:]),
                        waits=[(dve_s, vnorm)], inc=qx)
            ln_all = tile_done[-1]
            if DBG_SCRATCH:
                P.dma("sync", lambda e: e.dma_start(out=dln1t[:, :, :], in_=ln1T[:]),
                      waits=[(dve_s, ln_all)], inc=qx)
            _sections = {
                "v1": lambda: v_half(1, wbufB, "v1",
                    (lambda: load_w(wbufA, dwk, (0, 1024), "k0",
                                    wait=[(pe_s, pe_s.v)]))
                    if "k0" in SUBSET else None),
                "k0": lambda: qk_mh(dkh, wbufA, "k0", 0, range(5), 0,
                    (lambda: load_w(wbufB, dwk, (1024, 2048), "k1",
                                    wait=[(pe_s, pe_s.v)]))
                    if "k1" in SUBSET else None),
                "k1": lambda: qk_mh(dkh, wbufB, "k1", 1, range(5), 0,
                    (lambda: load_w(wbufA, dwq, (0, 1024), "q0",
                                    wait=[(pe_s, pe_s.v)]))
                    if "q0" in SUBSET else None),
                "q0": lambda: qk_mh(dqh, wbufA, "q0", 0, range(1, 5), 1,
                    (lambda: load_w(wbufB, dwq, (1024, 2048), "q1",
                                    wait=[(pe_s, pe_s.v)]))
                    if "q1" in SUBSET else None),
                "q1": lambda: qk_mh(dqh, wbufB, "q1", 1, range(1, 5), 1, None),
            }



            for sec in ("v1", "k0", "k1", "q0", "q1"):
                if sec in SUBSET:
                    _sections[sec]()

            if "bar" in SUBSET:
                P.barrier(esem, extra=dmas)

        if PHASE_LIMIT >= 3:
          with ExitStack() as e3:
            sb3 = lambda name, shape, dt: e3.enter_context(nc.sbuf_tensor(name, shape, dt))
            ps3 = lambda name, shape, dt: e3.enter_context(nc.psum_tensor(name, shape, dt))
            qbuf = sb3("qbuf", [128, H, 512], BF16)
            kbuf = sb3("kbuf", [128, H, 1024], BF16)
            vbuf = sb3("vbuf", [128, 8, D], BF16)
            mbuf = sb3("mbuf", [128, 8, 512], BF16)
            expT = [sb3(f"expT{i}", [128, 8, 512], BF16) for i in range(2)]
            rvec0 = [sb3(f"rvec0{i}", [1, 512], F32) for i in range(2)]
            rvec = [sb3(f"rvec{i}", [1, 512], F32) for i in range(2)]
            rbs = [sb3(f"rbs{i}", [128, 512], F32) for i in range(2)]
            avstg = [sb3(f"avstg{i}", [128, 512], BF16) for i in range(2)]
            SA = ps3("SA", [128, 4, 512], F32)
            SB = ps3("SB", [128, 4, 512], F32)

            qkv_done = [ (s0, s0.v), (s1, s1.v), (s2, s2.v), (s3, s3.v) ]
            av_dma = [0, 0]
            ps_free = [(act_s, 0), (act_s, 0)]
            tail_free = [(dve_s, 0), (dve_s, 0)]
            pend2 = []
            exp_prev = [(act_s, 0), (act_s, 0)]
            ln_prev = [(act_s, 0), (act_s, 0)]

            def emit_tail(pi, PS, last_av, vl4, h, qb):
                vbc = P.op("tensor", lambda e, PS=PS, pi=pi: e.matmul(
                    PS[:, 2, :], ones32[:], rvec[pi][:], start=True, stop=True),
                    waits=[(act_s, vl4)], inc=pe_s)
                vcp = P.op("scalar", lambda e, PS=PS, pi=pi: e.copy(
                    rbs[pi][:], PS[:, 2, :]),
                    waits=[(pe_s, vbc)], inc=act_s)
                ps_free[pi] = (act_s, vcp)
                vav = P.op("vector", lambda e, PS=PS, pi=pi: e.tensor_mul(
                    avstg[pi][:], PS[:, 0, :], rbs[pi][:]),
                    waits=[(pe_s, last_av), (act_s, vcp),
                           (s0 if pi == 0 else s1, av_dma[pi])],
                    inc=dve_s)
                av_dma[pi] = P.dma("scalar", lambda e, h=h, qb=qb, pi=pi:
                    e.dma_start(out=davh[h, :, qb*512:qb*512+512],
                                in_=avstg[pi][:]),
                    waits=[(dve_s, vav)], inc=(s0 if pi == 0 else s1))
                tail_free[pi] = (dve_s, vav)

            for qb in range(min(QB, P3_QB)):
                prev_pe = pe_s.v
                w0_ = qkv_done + [(pe_s, prev_pe)]
                for h in range(H):
                    P.dma("sync", lambda e, h=h, qb=qb: e.dma_start(
                        out=qbuf[:, h, :], in_=dqh[h, :, qb*512:qb*512+512]),
                        waits=w0_, inc=qx)
                    P.dma("sync", lambda e, h=h, qb=qb: e.dma_start(
                        out=kbuf[:, h, :], in_=dkh[h, :, qb*512:qb*512+1024]),
                        waits=w0_, inc=qx)
                for j in range(8):
                    P.dma("sync", lambda e, j=j, qb=qb: e.dma_start(
                        out=vbuf[:, j, :], in_=dvh[qb*4 + j, :, :]),
                        waits=w0_, inc=qx)
                ld_done = 0
                for mh in range(2):
                    ld_done = P.dma("sync", lambda e, qb=qb, mh=mh: e.dma_start(
                        out=mbuf[:, mh*4:(mh+1)*4, :],
                        in_=dmask[qb, mh*512:(mh+1)*512, :]
                            .rearrange("(jt p) i -> p jt i", p=128)),
                        waits=w0_, inc=qx)
                for hp in range(H // 2):
                    duo = (2 * hp, 2 * hp + 1)
                    if len(pend2) == 2:
                        for t_ in pend2:
                            emit_tail(*t_)
                        pend2 = []
                    st0 = {}
                    exp0 = {}
                    for h in duo:
                        pi = h % 2
                        PS = SA if pi == 0 else SB
                        eT = expT[pi]
                        last_mm = 0
                        for j4 in (3, 1, 2, 0):
                            if j4 == 3:
                                wts = [(qx, ld_done), exp_prev[pi],
                                       (qc, const_ready)]
                            elif j4 == 1:
                                wts = [ln_prev[pi]]
                            elif j4 == 2:
                                wts = [ps_free[pi]]
                            else:
                                wts = [tail_free[pi]]
                            P.op("tensor", lambda e, PS=PS, j4=j4, h=h:
                                e.matmul(PS[:, j4, :],
                                         kbuf[:, h, j4*128:j4*128+128],
                                         qbuf[:, h, :], start=True, stop=False),
                                waits=wts)
                            last_mm = P.op("tensor", lambda e, PS=PS, j4=j4:
                                e.matmul(PS[:, j4, :], id16[:], mbuf[:, j4, :],
                                         start=False, stop=True),
                                inc=(pe_s if j4 == 0 else None))
                        st0[h] = last_mm
                        exp0[h] = P.op("scalar", lambda e, PS=PS, eT=eT:
                            e.activation(eT[:, 0:4, :], PS[:, :, :],
                                         AF.Exp, scale=SCALE),
                            waits=[(pe_s, last_mm)], inc=act_s)
                    exp1 = {}
                    for h in duo:
                        pi = h % 2
                        PS = SA if pi == 0 else SB
                        eT = expT[pi]
                        last_mm = 0
                        for j4 in range(4):
                            j = 4 + j4
                            P.op("tensor", lambda e, PS=PS, j4=j4, j=j, h=h:
                                e.matmul(PS[:, j4, :],
                                         kbuf[:, h, j*128:j*128+128],
                                         qbuf[:, h, :], start=True, stop=False),
                                waits=([(act_s, exp0[h])] if j4 == 0 else []))
                            last_mm = P.op("tensor", lambda e, PS=PS, j4=j4, j=j:
                                e.matmul(PS[:, j4, :], id16[:], mbuf[:, j, :],
                                         start=False, stop=True),
                                inc=(pe_s if j4 == 3 else None))
                        exp1[h] = P.op("scalar", lambda e, PS=PS, eT=eT:
                            e.activation(eT[:, 4:8, :], PS[:, :, :],
                                         AF.Exp, scale=SCALE),
                            waits=[(pe_s, last_mm)], inc=act_s)
                        exp_prev[pi] = (act_s, exp1[h])
                    for h in duo:
                        pi = h % 2
                        PS = SA if pi == 0 else SB
                        eT = expT[pi]
                        last_av = 0
                        for j in range(8):
                            last_av = P.op("tensor",
                                lambda e, PS=PS, eT=eT, j=j, h=h:
                                e.matmul(PS[:, 0, :], vbuf[:, j, h*128:h*128+128],
                                         eT[:, j, :], start=(j == 0),
                                         stop=(j == 7)),
                                waits=([(act_s, exp1[h])] if j == 0 else []),
                                inc=(pe_s if j == 7 else None))
                        last_se = 0
                        for j in range(8):
                            last_se = P.op("tensor", lambda e, PS=PS, eT=eT, j=j:
                                e.matmul(PS[0:1, 1, :], ones16[:], eT[:, j, :],
                                         start=(j == 0), stop=(j == 7)),
                                inc=(pe_s if j == 7 else None))
                        vl3 = P.op("scalar", lambda e, PS=PS, pi=pi: e.activation(
                            rvec0[pi][:], PS[0:1, 1, :], AF.Ln),
                            waits=[(pe_s, last_se)], inc=act_s)
                        ln_prev[pi] = (act_s, vl3)
                        vl4 = P.op("scalar", lambda e, pi=pi: e.activation(
                            rvec[pi][:], rvec0[pi][:], AF.Exp, scale=-1.0),
                            waits=[(act_s, vl3)], inc=act_s)
                        pend2.append((pi, PS, last_av, vl4, h, qb))
                for t_ in pend2:
                    emit_tail(*t_)
                pend2 = []

            P.barrier(esem, extra=dmas)

        if PHASE_LIMIT >= 4:
          with ExitStack() as e4:
            sb4 = lambda name, shape, dt: e4.enter_context(nc.sbuf_tensor(name, shape, dt))
            ps4 = lambda name, shape, dt: e4.enter_context(nc.psum_tensor(name, shape, dt))
            wob = sb4("wob", [128, KT, D], BF16)
            avb = [sb4(f"avb{i}", [128, KT, 512], BF16) for i in range(2)]
            xst = [sb4(f"xst{i}", [128, D], F32) for i in range(2)]
            xT = sb4("xT", [128, KT, 512], F32)
            ost = [sb4(f"ost{i}", [128, 512], F32) for i in range(2)]
            pX = ps4("pX", [128, 4, 128], F32)
            pW = [ps4(f"pW{i}", [128, 512], F32) for i in range(2)]

            for k in range(KT):
                P.dma("gpsimd", lambda e, k=k: e.dma_start(
                    out=wob[:, k, :], in_=dwo[k*128:(k+1)*128, :]), inc=qmisc)
            wo_ready = qmisc.v
            av_done = [(s0, s0.v), (s1, s1.v)]

            ost_dma = [0, 0]
            pw_free = [(dve_s, 0), (dve_s, 0)]
            px_free = (dve_s, 0)
            avb_free = [0, 0]
            xst_free = [0, 0]
            for c in range(4):
                cb = c % 2
                avq = a0 if cb == 0 else a1
                avl = 0
                for q4 in range(4):
                    avl = P.dma("sync", lambda e, c=c, cb=cb, q4=q4: e.dma_start(
                        out=avb[cb][:, q4*4:(q4+1)*4, :],
                        in_=davh[q4*4:(q4+1)*4, :, c*512:c*512+512]
                            .rearrange("h p t -> p h t")),
                        waits=av_done + [(pe_s, avb_free[cb])], inc=avq)
                for t4 in range(4):
                    tb = t4 % 2
                    xq = l0 if tb == 0 else l1
                    xl = P.dma("sync", lambda e, c=c, t4=t4, tb=tb: e.dma_start(
                        out=xst[tb][:],
                        in_=dx[512 + c*512 + t4*128: 512 + c*512 + t4*128 + 128, :]),
                        waits=[(pe_s, xst_free[tb])], inc=xq)
                    for fb in range(4):
                        vtr = 0
                        for f4 in range(4):
                            f = fb * 4 + f4
                            vtr = P.op("tensor", lambda e, tb=tb, f=f, f4=f4:
                                e.transpose(pX[:, f4, :],
                                            xst[tb][:, f*128:f*128+128], id32[:]),
                                waits=([(xq, xl), px_free] if f4 == 0 else []),
                                inc=(pe_s if f4 == 3 else None))
                        vev = P.op("vector", lambda e, fb=fb, t4=t4: e.tensor_copy(
                            xT[:, fb*4:fb*4+4, t4*128:t4*128+128], pX[:, :, :]),
                            waits=[(pe_s, vtr)], inc=dve_s)
                        px_free = (dve_s, vev)
                    xst_free[tb] = pe_s.v
                xT_ready = px_free[1]
                for o in range(KT):
                    ob = o % 2
                    last_mm = 0
                    for k in range(KT):
                        last_mm = P.op("tensor", lambda e, o=o, k=k, cb=cb, ob=ob:
                            e.matmul(pW[ob][:], wob[:, k, o*128:o*128+128],
                                     avb[cb][:, k, :], start=(k == 0),
                                     stop=(k == KT-1)),
                            waits=([(qmisc, wo_ready), (avq, avl), pw_free[ob]]
                                   if k == 0 else []),
                            inc=(pe_s if k == KT-1 else None))
                    osq = s0 if ob == 0 else s1
                    vev = P.op("vector", lambda e, o=o, ob=ob: e.tensor_add(
                        ost[ob][:], pW[ob][:], xT[:, o, :]),
                        waits=[(pe_s, last_mm), (osq, ost_dma[ob]),
                               (dve_s, xT_ready)], inc=dve_s)
                    pw_free[ob] = (dve_s, vev)
                    ost_dma[ob] = P.dma("scalar", lambda e, o=o, c=c, ob=ob:
                        e.dma_start(out=dxatt[o, :, c*512:c*512+512],
                                    in_=ost[ob][:]),
                        waits=[(dve_s, vev)], inc=osq)
                avb_free[cb] = pe_s.v

            P.barrier(esem, extra=dmas)

        if PHASE_LIMIT >= 5:
          with ExitStack() as e5:
            sb5 = lambda name, shape, dt: e5.enter_context(nc.sbuf_tensor(name, shape, dt))
            ps5 = lambda name, shape, dt: e5.enter_context(nc.psum_tensor(name, shape, dt))
            acc = sb5("acc", [128, KT, GT], F32)
            ybuf = sb5("ybuf", [128, KT, GT], BF16)
            hbuf = sb5("hbuf", [128, ITK, GT], BF16)
            guw = sb5("guw", [128, KT, GUT], BF16)
            dww = sb5("dww", [128, ITK, D], BF16)
            cesb = sb5("cesb", [128, GT], F32)
            t1 = sb5("t1", [128, GT], F32)
            gtb = sb5("gtb", [128, KT, 32], BF16)
            aux = sb5("aux", [32, 3, GT], F32)
            lgT = sb5("lgT", [128, 256], F32)
            l8 = sb5("l8", [128, 8], F32)
            e8b = sb5("e8b", [128, 8], F32)
            cbuf = sb5("cbuf", [128, 8], F32)
            csh = sb5("csh", [128, 8], F32)
            kfb = sb5("kfb", [128, 8], F32)
            selb = sb5("selb", [128, 8], F32)
            zer8 = sb5("zer8", [128, 8], F32)
            sjnk = sb5("sjnk", [128, 2], F32)
            m32b = sb5("m32b", [128, 32], F32)
            e32b = sb5("e32b", [128, 32], F32)
            w32b = sb5("w32b", [128, 32], F32)
            scl = sb5("scl", [128, 4], F32)
            comb = sb5("comb", [128, 256], F32)
            outst = sb5("outst", [128, 512], F32)
            pg = [ps5(f"pg{i}", [128, GT], F32) for i in range(2)]
            pu = [ps5(f"pu{i}", [128, GT], F32) for i in range(2)]

            P.op("vector", lambda e: e.memset(zer8[:], 0.0))
            P.op("vector", lambda e: e.memset(sjnk[:], 0.0))
            for k in range(KT):
                P.dma("gpsimd", lambda e, k=k: e.dma_start(
                    out=gtb[:, k, :], in_=dgate[k*128:(k+1)*128, :]), inc=qmisc)
            gate_ready = qmisc.v

            guw_ready = 0
            dww_ready = 0
            last_gu_mm = 0
            last_dn_mm = 0

            def load_gu(e_):
                nonlocal guw_ready
                for k in range(KT):
                    guw_ready = P.dma("gpsimd", lambda en, e_=e_, k=k: en.dma_start(
                        out=guw[:, k, :], in_=dgu[e_, k*128:(k+1)*128, :]),
                        waits=[(pe_s, last_gu_mm)], inc=w0)

            def load_dw(e_):
                nonlocal dww_ready
                for k in range(ITK):
                    dww_ready = P.dma("gpsimd", lambda en, e_=e_, k=k: en.dma_start(
                        out=dww[:, k, :], in_=ddw[e_, k*128:(k+1)*128, :]),
                        waits=[(pe_s, last_dn_mm)], inc=w1)

            load_gu(0)
            load_dw(0)

            free_ps = {}
            hready = [(dve_s, 0)] * ITK
            ce_free = (dve_s, 0)
            t1_free = (dve_s, 0)
            acc_free_pe = 0
            y_done = 0
            out_dma = 0
            comb_st = 0

            for g in range(NG):
                acc_ld = 0
                for k in range(KT):
                    acc_ld = P.dma("sync", lambda e, k=k, g=g: e.dma_start(
                        out=acc[:, k, :], in_=dxatt[k, :, g*GT:(g+1)*GT]),
                        waits=[(pe_s, acc_free_pe)], inc=qx)
                last_st = 0
                sq_pe = 0
                for k in range(KT):
                    P.op("vector", lambda e, k=k: e.tensor_copy(
                        ybuf[:, k, :], acc[:, k, :]),
                        waits=[(qx, acc_ld), (pe_s, last_gu_mm)],
                        inc=(dve_s if k == KT-1 else None))
                    vsq = P.op("scalar", lambda e, k=k: e.activation(
                        t1[:], acc[:, k, :], AF.Square),
                        waits=[(qx, acc_ld), (pe_s, sq_pe), t1_free], inc=act_s)
                    for hf in range(2):
                        P.op("tensor", lambda e, k=k, hf=hf: e.matmul(
                            pg[0][0:1, hf*512:hf*512+512], ones32c[:],
                            acc[:, k, hf*512:hf*512+512],
                            start=(k == 0), stop=(k == KT-1)),
                            waits=[(qx, acc_ld)] +
                                  ([free_ps.get(id(pg[0]), (dve_s, 0))]
                                   if k == 0 and hf == 0 else []))
                        v_ = P.op("tensor", lambda e, k=k, hf=hf: e.matmul(
                            pu[0][0:1, hf*512:hf*512+512], ones32c[:],
                            t1[:, hf*512:hf*512+512],
                            start=(k == 0), stop=(k == KT-1)),
                            waits=[(act_s, vsq)] +
                                  ([free_ps.get(id(pu[0]), (dve_s, 0))]
                                   if k == 0 and hf == 0 else []),
                            inc=(pe_s if hf == 1 else None))
                        if hf == 1:
                            last_st = v_
                    sq_pe = last_st
                ycp_done = dve_s.v
                P.op("vector", lambda e: e.tensor_scalar(
                    aux[0:1, 0, :], pg[0][0:1, 0:GT], 1.0 / D, None, ALU.mult),
                    waits=[(pe_s, last_st)])
                P.op("vector", lambda e: e.tensor_scalar(
                    aux[0:1, 2, :], pu[0][0:1, 0:GT], 1.0 / D, 1e-5,
                    ALU.mult, ALU.add))
                P.op("vector", lambda e: e.drain())
                P.op("vector", lambda e: e.tensor_mul(
                    aux[0:1, 1, :], aux[0:1, 0, :], aux[0:1, 0, :]))
                P.op("vector", lambda e: e.drain())
                vvar = P.op("vector", lambda e: e.tensor_sub(
                    aux[0:1, 1, :], aux[0:1, 2, :], aux[0:1, 1, :]), inc=dve_s)
                free_ps[id(pg[0])] = (dve_s, vvar)
                free_ps[id(pu[0])] = (dve_s, vvar)
                vln2 = P.op("scalar", lambda e: e.activation(
                    aux[0:1, 2, :], aux[0:1, 1, :], AF.Ln),
                    waits=[(dve_s, vvar)], inc=act_s)
                vrs = P.op("scalar", lambda e: e.activation(
                    aux[0:1, 1, :], aux[0:1, 2, :], AF.Exp, scale=-0.5),
                    waits=[(act_s, vln2)], inc=act_s)
                vbb = P.op("vector", lambda e: e.scalar_tensor_tensor(
                    aux[0:1, 2, :], aux[0:1, 0, :], -1.0, aux[0:1, 1, :],
                    ALU.mult, ALU.mult), waits=[(act_s, vrs)], inc=dve_s)
                vbc = 0
                for hf in range(2):
                    P.op("tensor", lambda e, hf=hf: e.matmul(
                        pg[1][:, hf*512:hf*512+512], ones32[:],
                        aux[0:1, 1, hf*512:hf*512+512], start=True, stop=True),
                        waits=[(dve_s, vbb)] +
                              ([free_ps.get(id(pg[1]), (dve_s, 0))] if hf == 0 else []))
                    vbc = P.op("tensor", lambda e, hf=hf: e.matmul(
                        pu[1][:, hf*512:hf*512+512], ones32[:],
                        aux[0:1, 2, hf*512:hf*512+512], start=True, stop=True),
                        waits=([free_ps.get(id(pu[1]), (dve_s, 0))] if hf == 0 else []),
                        inc=(pe_s if hf == 1 else None))
                for k in range(KT):
                    P.op("vector", lambda e, k=k: e.tensor_mul(
                        t1[:], acc[:, k, :], pg[1][:]),
                        waits=([(pe_s, vbc), t1_free, (pe_s, last_st)]
                               if k == 0 else []))
                    P.op("vector", lambda e: e.drain())
                    y_done = P.op("vector", lambda e, k=k: e.tensor_add(
                        ybuf[:, k, :], t1[:], pu[1][:]), inc=dve_s)
                    P.op("vector", lambda e: e.drain())
                free_ps[id(pg[1])] = (dve_s, y_done)
                free_ps[id(pu[1])] = (dve_s, y_done)
                t1_free = (dve_s, y_done)
                last_lg = 0
                for hf in range(2):
                    for k in range(KT):
                        last_lg = P.op("tensor", lambda e, hf=hf, k=k: e.matmul(
                            pg[0][0:32, hf*512:hf*512+512], gtb[:, k, :],
                            ybuf[:, k, hf*512:hf*512+512],
                            start=(k == 0), stop=(k == KT-1)),
                            waits=([(qmisc, gate_ready), (dve_s, y_done),
                                    free_ps.get(id(pg[0]), (dve_s, 0))]
                                   if k == 0 and hf == 0 else []),
                            inc=(pe_s if (k == KT-1 and hf == 1) else None))
                vlg = P.op("scalar", lambda e: e.copy(aux[0:32, 0, :], pg[0][0:32, 0:GT]),
                           waits=[(pe_s, last_lg), (qx, comb_st)], inc=act_s)
                free_ps[id(pg[0])] = (act_s, vlg)
                vtr = 0
                for c8 in range(8):
                    vtr = P.op("tensor", lambda e, c8=c8: e.transpose(
                        pu[0][:, c8*32:c8*32+32], aux[0:32, 0, c8*128:c8*128+128],
                        id32[0:32, 0:32]),
                        waits=([(act_s, vlg),
                                free_ps.get(id(pu[0]), (dve_s, 0))]
                               if c8 == 0 else []),
                        inc=(pe_s if c8 == 7 else None))
                vlgt = P.op("vector", lambda e: e.tensor_copy(
                    lgT[:], pu[0][:, 0:256]),
                    waits=[(pe_s, vtr)], inc=dve_s)
                free_ps[id(pu[0])] = (dve_s, vlgt)
                comb_last = 0
                rt_reader = 0
                def SPC():
                    P.op("vector", lambda e: e.drain())
                for c8 in range(8):
                    P.op("vector", lambda e, c8=c8: e.max(
                        l8[:], lgT[:, c8*32:c8*32+32]),
                        waits=[(dve_s, vlgt), (act_s, rt_reader)])
                    SPC()
                    vng = P.op("vector", lambda e: e.tensor_scalar(
                        scl[:, 0:1], l8[:, 0:1], -1.0, None, ALU.mult), inc=dve_s)
                    ve8 = P.op("scalar", lambda e: e.activation(
                        e8b[:], l8[:], AF.Exp, bias=scl[:, 0:1]),
                        waits=[(dve_s, vng)], inc=act_s)
                    P.op("vector", lambda e: e.tensor_tensor_scan(
                        cbuf[:], e8b[:], zer8[:], 0.0, ALU.add, ALU.add),
                        waits=[(act_s, ve8)])
                    SPC()
                    P.op("vector", lambda e: e.tensor_scalar(
                        scl[:, 1:2], cbuf[:, 7:8], 0.8, None, ALU.mult))
                    P.op("vector", lambda e: e.memset(csh[:, 0:1], 0.0))
                    P.op("vector", lambda e: e.tensor_copy(
                        csh[:, 1:8], cbuf[:, 0:7]))
                    SPC()
                    P.op("vector", lambda e: e.tensor_single_scalar(
                        kfb[:], csh[:], scl[:, 1:2], ALU.is_lt))
                    SPC()
                    P.op("vector", lambda e: e.tensor_scalar(
                        selb[:], kfb[:], -1e30, 1e30, ALU.mult, ALU.add))
                    SPC()
                    P.op("vector", lambda e: e.tensor_add(
                        selb[:], selb[:], l8[:]))
                    SPC()
                    P.op("vector", lambda e: e.tensor_reduce(
                        scl[:, 2:3], selb[:], AX.X, ALU.min))
                    SPC()
                    SPC()
                    vms = P.op("vector", lambda e, c8=c8: e.tensor_single_scalar(
                        m32b[:], lgT[:, c8*32:c8*32+32], scl[:, 2:3], ALU.is_ge),
                        inc=dve_s)
                    ve32 = P.op("scalar", lambda e, c8=c8: e.activation(
                        e32b[:], lgT[:, c8*32:c8*32+32], AF.Exp, bias=scl[:, 0:1]),
                        waits=[(dve_s, vms)], inc=act_s)
                    rt_reader = ve32
                    SPC()
                    P.op("vector", lambda e: e.tensor_mul(
                        w32b[:], e32b[:], m32b[:]), waits=[(act_s, ve32)])
                    SPC()
                    P.op("vector", lambda e: e.reduce_sum(
                        scl[:, 3:4], w32b[:], AX.X))
                    SPC()
                    P.op("vector", lambda e: e.reciprocal(
                        scl[:, 3:4], scl[:, 3:4]))
                    SPC()
                    SPC()
                    comb_last = P.op("vector", lambda e, c8=c8: e.tensor_scalar(
                        comb[:, c8*32:c8*32+32], w32b[:], scl[:, 3:4], None,
                        ALU.mult), inc=dve_s)
                def emit_comb_store(g=None):
                    nonlocal comb_st
                    vtr2 = 0
                    for c8 in range(8):
                        pdst = pg[1] if c8 < 4 else pu[1]
                        vtr2 = P.op("tensor", lambda e, c8=c8, pdst=pdst: e.transpose(
                            pdst[0:32, (c8 % 4)*128:(c8 % 4)*128+128],
                            comb[:, c8*32:c8*32+32], id32[:]),
                            waits=([(dve_s, comb_last),
                                    free_ps.get(id(pg[1]), (dve_s, 0)),
                                    free_ps.get(id(pu[1]), (dve_s, 0))]
                                   if c8 == 0 else []),
                            inc=(pe_s if c8 == 7 else None))
                    vct = P.op("scalar", lambda e: e.copy(
                        aux[0:32, 0, 0:512], pg[1][0:32, 0:512]),
                        waits=[(pe_s, vtr2)], inc=act_s)
                    vct2 = P.op("scalar", lambda e: e.copy(
                        aux[0:32, 0, 512:1024], pu[1][0:32, 0:512]), inc=act_s)
                    free_ps[id(pg[1])] = (act_s, vct)
                    free_ps[id(pu[1])] = (act_s, vct2)
                    comb_st = P.dma("sync", lambda e, g=g: e.dma_start(
                        out=dcombt[g, :, :], in_=aux[0:32, 0, :]),
                        waits=[(act_s, vct2)], inc=qx)

                for e_ in range(NE):
                    if e_ > 0:
                        ce_ready = P.dma("sync", lambda en, g=g, e_=e_: en.dma_start(
                            out=cesb[:],
                            in_=dcombt[g, e_:e_+1, :].partition_broadcast(128)),
                            waits=[ce_free, (qx, comb_st)], inc=qce)
                    for jp in range(ITK):
                        p = jp & 1
                        for tile, col0 in ((pg[p], jp*128), (pu[p], 768 + jp*128)):
                            wts0 = [free_ps.get(id(tile), (dve_s, 0)),
                                    (w0, guw_ready), (dve_s, y_done)]
                            first = True
                            for k in range(KT):
                                for hf in range(2):
                                    v = P.op("tensor",
                                        lambda en, tile=tile, col0=col0, k=k, hf=hf:
                                        en.matmul(tile[:, hf*512:hf*512+512],
                                                  guw[:, k, col0:col0+128],
                                                  ybuf[:, k, hf*512:hf*512+512],
                                                  start=(k == 0), stop=(k == KT-1)),
                                        waits=(wts0 if first else ()),
                                        inc=(pe_s if (k == KT-1 and hf == 1)
                                             else None))
                                    if k == KT-1 and hf == 1:
                                        last_gu_mm = v
                                    first = False
                        if e_ == 0 and jp == 0:
                            emit_comb_store(g)
                            ce_ready = P.dma("sync", lambda en, g=g: en.dma_start(
                                out=cesb[:],
                                in_=dcombt[g, 0:1, :].partition_broadcast(128)),
                                waits=[ce_free, (qx, comb_st)], inc=qce)
                        va = P.op("scalar", lambda en, p=p: en.activation(
                            t1[:], pg[p][:], AF.Silu),
                            waits=[(pe_s, last_gu_mm), t1_free], inc=act_s)
                        free_ps[id(pg[p])] = (act_s, va)
                        vd = P.op("vector", lambda en, p=p: en.tensor_mul(
                            t1[:], t1[:], pu[p][:]),
                            waits=[(act_s, va)], inc=dve_s)
                        free_ps[id(pu[p])] = (dve_s, vd)
                        vh = P.op("vector", lambda en, jp=jp: en.tensor_mul(
                            hbuf[:, jp, :], t1[:], cesb[:]),
                            waits=[(qce, ce_ready), (pe_s, last_dn_mm)], inc=dve_s)
                        t1_free = (dve_s, vh)
                        hready[jp] = (dve_s, vh)
                    ce_free = hready[ITK - 1]
                    if e_ + 1 < NE:
                        load_gu(e_ + 1)
                    pseq = [pg[0], pu[0], pg[1], pu[1]]
                    for o in range(KT):
                        tile = pseq[o % 4]
                        wts0 = [free_ps.get(id(tile), (dve_s, 0)),
                                (w1, dww_ready)]
                        first = True
                        for k in range(ITK):
                            for hf in range(2):
                                wts = list(wts0) if first else []
                                wts.append(hready[k])
                                v = P.op("tensor",
                                    lambda en, tile=tile, o=o, k=k, hf=hf:
                                    en.matmul(tile[:, hf*512:hf*512+512],
                                              dww[:, k, o*128:o*128+128],
                                              hbuf[:, k, hf*512:hf*512+512],
                                              start=(k == 0), stop=(k == ITK-1)),
                                    waits=wts,
                                    inc=(pe_s if (k == ITK-1 and hf == 1)
                                         else None))
                                if k == ITK-1 and hf == 1:
                                    last_dn_mm = v
                                first = False
                        vacc = P.op("vector", lambda en, tile=tile, o=o:
                            en.tensor_add(acc[:, o, :], acc[:, o, :], tile[:]),
                            waits=[(pe_s, last_dn_mm)], inc=dve_s)
                        free_ps[id(tile)] = (dve_s, vacc)
                    if e_ + 1 < NE:
                        load_dw(e_ + 1)
                    elif g + 1 < NG:
                        load_gu(0)
                        load_dw(0)
                acc_done = dve_s.v
                for tt in range(8):
                    for fb in range(4):
                        pb = pg[fb % 2]
                        vtr = 0
                        for f4 in range(4):
                            f = fb * 4 + f4
                            vtr = P.op("tensor", lambda e, f=f, f4=f4, tt=tt, pb=pb:
                                e.transpose(pb[:, f4*128:f4*128+128],
                                            acc[:, f, tt*128:tt*128+128], id32[:]),
                                waits=([(dve_s, acc_done),
                                        free_ps.get(id(pb), (dve_s, 0))]
                                       if f4 == 0 else []),
                                inc=(pe_s if f4 == 3 else None))
                        vev = P.op("vector", lambda e, pb=pb: e.tensor_copy(
                            outst[:], pb[:, 0:512]),
                            waits=[(pe_s, vtr), (qout, out_dma)], inc=dve_s)
                        free_ps[id(pb)] = (dve_s, vev)
                        out_dma = P.dma("sync", lambda e, g=g, tt=tt, fb=fb:
                            e.dma_start(out=dout[g*GT + tt*128: g*GT + tt*128 + 128,
                                                 fb*512:fb*512+512],
                                        in_=outst[:]),
                            waits=[(dve_s, vev)], inc=qout)
                acc_free_pe = pe_s.v

        P.barrier(esem, extra=dmas)

        P.realize(blk)
    return nc


_NC_CACHE = {}


def kernel(L0_attn_x, L0_attn_Wq, L0_attn_Wk, L0_attn_Wv, L0_attn_Wo,
           L0_ffn_gate_weight, L0_ffn_gate_up_proj, L0_ffn_down_proj):
    x = np.ascontiguousarray(np.asarray(L0_attn_x, dtype=np.float32))
    B, S, _ = x.shape
    idx = np.empty(D, np.int64)
    for h in range(H):
        base = h * 128
        idx[base:base+64] = base + 2 * np.arange(64)
        idx[base+64:base+128] = base + 2 * np.arange(64) + 1
    wq = np.ascontiguousarray(np.asarray(L0_attn_Wq).T[:, idx], np.float32)
    wk = np.ascontiguousarray(np.asarray(L0_attn_Wk).T[:, idx], np.float32)
    wv = np.ascontiguousarray(np.asarray(L0_attn_Wv).T, np.float32)
    wo = np.ascontiguousarray(np.asarray(L0_attn_Wo).T, np.float32)
    gate = np.ascontiguousarray(np.asarray(L0_ffn_gate_weight).T, np.float32)
    gu = np.ascontiguousarray(
        np.asarray(L0_ffn_gate_up_proj).transpose(0, 2, 1), np.float32)
    dw = np.ascontiguousarray(
        np.asarray(L0_ffn_down_proj).transpose(0, 2, 1), np.float32)
    inv_freq = 1.0 / 10000.0 ** (np.arange(0, 128, 2, dtype=np.float64) / 128.0)
    ident32 = np.eye(128, dtype=np.float32)
    ident16 = np.eye(128).astype(ml_dtypes.bfloat16)
    ones16 = np.ones((128, 1), ml_dtypes.bfloat16)
    ones32 = np.ones((1, 128), np.float32)
    i_ = np.arange(512)[None, :]
    j_ = np.arange(1024)[:, None]
    base_m = (j_ > i_) & (j_ <= i_ + 512)
    shift = ESHIFT / SCALE
    m_full = np.where(base_m, shift, -1e30).astype(np.float32)
    m_n0 = np.where(base_m & (j_ >= 512), shift, -1e30).astype(np.float32)

    nc = _NC_CACHE.get("nc")
    if nc is None:
        nc = build()
        _NC_CACHE["nc"] = nc

    in_maps = []
    for c in range(8):
        seq, half = c // 2, c % 2
        own = x[seq, half*TOK:(half+1)*TOK]
        halo = (np.zeros((512, D), np.float32) if half == 0
                else x[seq, TOK-512:TOK])
        x_all = np.ascontiguousarray(np.concatenate([halo, own], axis=0))
        pos = np.maximum(half*TOK - 512 + np.arange(TOKH), 0).astype(np.float64)
        ang = pos[None, :] * inv_freq[:, None]
        ropec = np.concatenate([np.cos(ang)] * 2, 0).astype(np.float32)
        ropes = np.concatenate([np.sin(ang)] * 2, 0).astype(np.float32)
        masks = np.stack([m_n0 if (half == 0 and qb == 0) else m_full
                          for qb in range(QB)]).astype(ml_dtypes.bfloat16)
        in_maps.append({
            "x": x_all, "wq": wq, "wk": wk, "wv": wv, "wo": wo, "gate": gate,
            "gu": gu, "dw": dw, "ropec": ropec, "ropes": ropes, "maskt": masks,
            "id16m": ident16, "id32m": ident32, "ones16": ones16,
            "ones32": ones32, "ones32c": np.ones((128, 1), np.float32),
        })
    res = run_bass_kernel_spmd(nc, in_maps, core_ids=list(range(8)))
    out = np.empty((B, S, D), np.float32)
    for c in range(8):
        seq, half = c // 2, c % 2
        out[seq, half*TOK:(half+1)*TOK] = res.results[c]["out"]
    return out
